# version 04
# baseline (speedup 1.0000x reference)
import os
import contextlib
import numpy as np
import ml_dtypes
import concourse.bass as bass
import concourse.mybir as mybir
from concourse.bass_utils import run_bass_kernel_spmd

F32 = mybir.dt.float32
BF16 = mybir.dt.bfloat16
FP8 = mybir.dt.float8e4
AF = mybir.ActivationFunctionType
ALU = mybir.AluOpType

HID = 2048
DIN = 8512
INTER = 4096
GN = 128
NH = 64
HD = 64
KC = 4
CONVD = INTER + 2 * GN
T = 2048
TH = T + 3
L = 128
NCH = T // L
NFB = HID // 128
NCHB_X = INTER // 128
NCHB_BC = CONVD // 128
NGATE = INTER // 128
FFH = 8192
NJB = FFH // 128
TT1 = 1024
EPS = 1e-5
GELU_C1 = float(np.sqrt(2.0 / np.pi))
GELU_C2 = 0.044715

TAPS = os.environ.get("KTAPS", "") == "1"
KSTOP = os.environ.get("KSTOP", "")
KLN_ID = os.environ.get("KLN_ID", "") == "1"
KCHB1 = os.environ.get("KCHB1", "") == "1"


def build_nc():
    nc = bass.Bass("TRN2", target_bir_lowering=False, debug=False)

    dai = lambda n, s, d=F32: nc.dram_tensor(n, s, d, kind="ExternalInput").ap()
    dao = lambda n, s, d=F32: nc.dram_tensor(n, s, d, kind="ExternalOutput").ap()
    dint = lambda n, s, d=F32: nc.dram_tensor(n, s, d, kind="Internal").ap()

    xh = dai("xh", [TH, HID])
    W_in = dai("W_in", [HID, DIN])
    Wg0 = dai("Wg0", [HID, FFH]); Wg1 = dai("Wg1", [HID, FFH])
    W2d = dai("W2d", [FFH, HID])
    Woutd = dai("Woutd", [INTER, HID])
    wconv = dai("wconv", [128, NCHB_BC * KC])
    convb = dai("convb", [128, NCHB_BC])
    dtbias = dai("dtbias", [NH, 1])
    Acol = dai("Acol", [NH, 1])
    Dbc_d = dai("Dbc_d", [128, NH], BF16)
    normw = dai("normw", [128, NCHB_X])
    cmask = dai("cmask", [64, 8])
    identf_d = dai("identf_d", [128, 128])
    identb_d = dai("identb_d", [128, 128], BF16)
    triu_d = dai("triu_d", [128, 128])
    ones128b_d = dai("ones128b_d", [128, 1], BF16)
    diffL_d = dai("diffL_d", [97, 2048])
    diffR_d = dai("diffR_d", [97, NH * L])

    out = dao("out", [T, HID])
    taps = {}
    if TAPS:
        taps["t_xlnT"] = dao("t_xlnT", [128, NFB * TH], BF16)
        taps["t_dt"] = dao("t_dt", [NH, TH])
        taps["t_B"] = dao("t_B", [128, T], BF16)
        taps["t_C"] = dao("t_C", [128, T], BF16)
        taps["t_y"] = dao("t_y", [128, INTER], BF16)
        taps["t_z"] = dao("t_z", [128, NCHB_X * 128], BF16)
        taps["t_x2"] = dao("t_x2", [128, HID])
        taps["t_glob"] = dao("t_glob", [128, INTER])
        taps["t_xbc"] = dao("t_xbc", [128, INTER], BF16)
        taps["t_st"] = dao("t_st", [128, 4 * 17])
        taps["t_carry"] = dao("t_carry", [128, INTER])
        taps["t_xw"] = dao("t_xw", [128, INTER], BF16)
        taps["t_dbc"] = dao("t_dbc", [128, NH])
        taps["t_cin"] = dao("t_cin", [129, INTER])
        taps["t_misc"] = dao("t_misc", [128, 80])
        taps["t_p6a"] = dao("t_p6a", [128, 1024], FP8)
        taps["t_p6b"] = dao("t_p6b", [128, 1024], FP8)

    sp_xT = dint("sp_xT", [NCH, 128, INTER], BF16)
    sp_carry = dint("sp_carry", [NCH, 128, INTER])
    sp_sg = dint("sp_sg", [NGATE, 128, T], BF16)
    sp_z = dint("sp_z", [NCHB_X, 128, T], BF16)
    sp_x2 = dint("sp_x2", [NCH, 128, HID])
    sp_x2lnT = dint("sp_x2lnT", [NFB, 128, T], FP8)
    wg0_bf = dint("wg0_bf", [HID, FFH], FP8)
    wg1_bf = dint("wg1_bf", [HID, FFH], FP8)
    w2_bf = dint("w2_bf", [FFH, HID], FP8)
    wout_bf = dint("wout_bf", [INTER, HID], BF16)
    cc_in = dint("cc_in", [129, INTER])
    cc_out = dint("cc_out", [4 * 129, INTER])

    Q = {k: [] for k in ("te", "ve", "ac", "gp", "sy")}

    class Cnt:
        def __init__(self):
            self.v = 0

    cS, cG, cT, cV, cA, cP, cC = (Cnt() for _ in range(7))
    ctr = {"sS": cS, "sG": cG, "sT": cT, "sV": cV, "sA": cA, "sP": cP, "sC": cC}
    qsem = {"te": ("sT", cT), "ve": ("sV", cV), "ac": ("sA", cA), "gp": ("sP", cP)}

    ctx = contextlib.ExitStack()
    sem = {name: ctx.enter_context(nc.semaphore(name)) for name in ctr}

    dirty_w = {"ve": set(), "gp": set()}
    dirty_r = {"ve": set(), "gp": set()}

    def _apname(a):
        try:
            return a.tensor.name
        except AttributeError:
            return None

    def op(q, meth, *args, **kw):
        if q in dirty_w:
            outn = _apname(args[0]) if args else None
            rds = {n for n in (_apname(a) for a in args[1:]) if n}
            for kwv in kw.values():
                n = _apname(kwv)
                if n:
                    rds.add(n)
            hazard = (rds & dirty_w[q]) or (outn in dirty_w[q]) or (outn in dirty_r[q])
            if hazard:
                sname, c = qsem[q]
                for item in reversed(Q[q]):
                    if item[0] in ("op", "cc"):
                        if sname not in item[-1]:
                            c.v += 1
                            item[-1].append(sname)
                        break
                Q[q].append(["wait", sname, c.v])
                dirty_w[q].clear()
                dirty_r[q].clear()
            if outn:
                dirty_w[q].add(outn)
            dirty_r[q] |= rds
        Q[q].append(["op", meth, args, kw, []])

    def wait(q, s, v):
        if v > 0:
            Q[q].append(["wait", s, v])

    def bump(q):
        s, c = qsem[q]
        for item in reversed(Q[q]):
            if item[0] in ("op", "cc"):
                if s not in item[-1]:
                    c.v += 1
                    item[-1].append(s)
                return c.v
        raise AssertionError(f"bump({q}) with no prior instruction")

    def dma(q, dst, src):
        s = "sS" if q == "sy" else "sG"
        c = ctr[s]
        c.v += 16
        Q[q].append(["dma", dst, src, s, []])
        return c.v

    guards = {}

    def used_at(buf, s, v):
        guards[buf] = [(s, v)]

    def used_also(buf, s, v):
        guards.setdefault(buf, []).append((s, v))

    def guard(q, buf):
        for g in guards.get(buf, ()):
            wait(q, g[0], g[1])

    sbuf_stack = []

    def sb(n, shape, d=F32):
        st = contextlib.ExitStack()
        t = st.enter_context(nc.sbuf_tensor(n, shape, d))
        sbuf_stack.append((n, st))
        return t

    def sbclose(*names):
        for nme in names:
            n2, st = sbuf_stack[-1]
            assert n2 == nme, f"sbuf close order: want {nme}, top is {n2}"
            st.close()
            sbuf_stack.pop()

    psum_stack = []

    def psopen(n, shape, d=F32):
        st = contextlib.ExitStack()
        t = st.enter_context(nc.psum_tensor(n, shape, d))
        psum_stack.append((n, st))
        return t

    def psclose(*names):
        for nme in names:
            n2, st = psum_stack[-1]
            assert n2 == nme, f"psum close order: want {nme}, top is {n2}"
            st.close()
            psum_stack.pop()

    def phase_gate():
        for q in ("sy", "gp", "ve", "te", "ac"):
            for sn, c in (("sT", cT), ("sV", cV), ("sA", cA), ("sP", cP),
                          ("sS", cS), ("sG", cG)):
                wait(q, sn, c.v)
        for q in ("ve", "gp"):
            dirty_w[q].clear()
            dirty_r[q].clear()

    identf = sb("identf", [128, 128])
    identb = sb("identb", [128, 128], BF16)
    triu_f = sb("triu_f", [128, 128])
    ones128b = sb("ones128b", [128, 1], BF16)
    ones64f = sb("ones64f", [64, 128])
    wconv_s = sb("wconv_s", [128, NCHB_BC * KC])
    convb_s = sb("convb_s", [128, NCHB_BC])
    convbh_s = sb("convbh_s", [128, NCHB_BC])
    dtbias_s = sb("dtbias_s", [NH, 1])
    Acol_s = sb("Acol_s", [NH, 1])
    Dbc_s = sb("Dbc_s", [128, NH], BF16)
    normw_s = sb("normw_s", [128, NCHB_X])
    cmask_s = sb("cmask_s", [64, 8])
    sq_scr = sb("sq_scr", [128, HID], BF16)
    dbg = sb("dbg", [128, 80]) if TAPS else None
    NTB = 17
    stat_m = sb("stat_m", [128, NTB]); stat_q = sb("stat_q", [128, NTB])
    rstd1 = sb("rstd1", [128, NTB]); bias1 = sb("bias1", [128, NTB])
    hscr1 = sb("hscr1", [128, NTB]); hscr2 = sb("hscr2", [128, NTB])
    dcols = sb("dcols", [NH, NCH + 1])
    Dprod = sb("Dprod", [NH, NCH + 1])
    varz = sb("varz", [128, NCH])
    rstdz = sb("rstdz", [128, NCH])
    stat2_m = sb("stat2_m", [128, NCH]); stat2_q = sb("stat2_q", [128, NCH])
    rstd2 = sb("rstd2", [128, NCH]); bias2 = sb("bias2", [128, NCH])
    dAbuf = sb("dAbuf", [NH, L])
    diagtmp = sb("diagtmp", [64, 64])
    vrow = sb("vrow", [1, 128])
    prow = sb("prow", [1, 64])
    pcol = sb("pcol", [NH, 4]); coefc = sb("coefc", [NH, 4]); cbc = sb("cbc", [128, NH])
    decayf = sb("decayf", [NH, L]); wstf = sb("wstf", [NH, L])
    wstTM = sb("wstTM", [128, NH], BF16)
    BtTM = sb("BtTM", [128, 128], BF16)
    dbc = sb("dbc", [128, NH])

    for dst, src in ((identf, identf_d), (identb, identb_d), (triu_f, triu_d),
                     (ones128b, ones128b_d), (wconv_s, wconv), (convb_s, convb),
                     (dtbias_s, dtbias), (Acol_s, Acol), (Dbc_s, Dbc_d),
                     (normw_s, normw), (cmask_s, cmask)):
        dma("sy", dst[:, :], src)
    v_const = cS.v
    wait("ve", "sS", v_const)
    op("ve", "memset", ones64f[:, :], 1.0)
    op("ve", "tensor_scalar_mul", convbh_s[:, :], convb_s[:, :], 0.5)
    op("ve", "memset", stat_q[:, :], 0.0)
    op("ve", "memset", stat2_q[:, :], 0.0)
    op("ve", "memset", stat_m[:, :], 0.0)
    op("ve", "memset", stat2_m[:, :], 0.0)
    v_const_ve = bump("ve")

    wcast = {}
    for nm, (dstw, srcw, rows) in {"wout": (wout_bf, Woutd, INTER),
                                   "wg0": (wg0_bf, Wg0, HID),
                                   "wg1": (wg1_bf, Wg1, HID),
                                   "w2": (w2_bf, W2d, FFH)}.items():
        for r0 in range(0, rows, 1024):
            dma("gp", dstw[r0:r0 + 1024, :], srcw[r0:r0 + 1024, :])
        wcast[nm] = cG.v

    def emit_heron(q, out_ap, var_ap, s1, s2, niter=5):
        BIG = 1.0e6
        op(q, "memset", out_ap, 1.0)
        for k in range(-17, 21):
            sc = float(2.0 ** (-k))
            op(q, "tensor_scalar", s1, var_ap, BIG * sc, -BIG, ALU.mult, ALU.add)
            op(q, "tensor_scalar_min", s2, s1, 1.0)
            op(q, "tensor_scalar_max", s1, s2, 0.0)
            op(q, "tensor_scalar", s2, s1, -(1.0 - 2.0 ** -0.5), 1.0, ALU.mult, ALU.add)
            op(q, "tensor_mul", out_ap, out_ap, s2)
        op(q, "tensor_scalar_mul", out_ap, out_ap, float(2.0 ** 8.5))
        for _ in range(niter):
            op(q, "tensor_mul", s1, out_ap, out_ap)
            op(q, "tensor_mul", s1, s1, var_ap)
            op(q, "tensor_scalar", s1, s1, -0.5, 1.5, ALU.mult, ALU.add)
            op(q, "tensor_mul", out_ap, out_ap, s1)

    B_FM = sb("B_FM", [128, T], BF16)
    C_FM = sb("C_FM", [128, T], BF16)
    dtv = sb("dtv", [NH, TH])
    diffL = sb("diffL", [97, 2048])
    dtTM = sb("dtTM", [128, NCH * NH], BF16)
    uall = sb("uall", [128, NCH * NH], BF16)
    xTb = [sb(f"xTb{i}", [128, INTER], BF16) for i in range(2)]
    globalC = sb("globalC", [128, INTER])

    xlnT = sb("xlnT", [128, NFB * TH], BF16)
    xtile = [sb(f"xtile{i}", [128, HID]) for i in range(2)]
    xln_tm = sb("xln_tm", [128, HID], BF16)

    p_trb = psopen("p_trb", [128, 128], BF16)
    p_trb2 = psopen("p_trb2", [128, 128], BF16)
    p_ms = psopen("p_ms", [128, 128])
    p_trf = psopen("p_trf", [128, 128])

    tb_ranges = [(128 * i, 128) for i in range(16)] + [(TH - 128, 128)]

    for i, (t0, tn) in enumerate(tb_ranges):
        buf = xtile[i % 2]
        guard("sy", f"xtile{i % 2}")
        w = dma("sy", buf[:, :], xh[t0:t0 + tn, :])
        wait("ve", "sS", w)
        op("ve", "reduce_sum", stat_m[:, i:i + 1], buf[:, :], mybir.AxisListType.X)
        vv_r = bump("ve")
        wait("ac", "sS", w)
        wait("ac", "sV", v_const_ve)
        op("ac", "activation", sq_scr[:, :], buf[:, :], AF.Square, accum_out=stat_q[:, i:i + 1])
        va = bump("ac")
        used_at(f"xtile{i % 2}", "sA", va)
        used_also(f"xtile{i % 2}", "sV", vv_r)
    v_stats_a = cA.v
    v_stats_v = cV.v

    wait("ve", "sA", v_stats_a)
    wait("ve", "sV", v_stats_v)
    op("ve", "tensor_scalar_mul", stat_m[:, :], stat_m[:, :], 1.0 / HID)
    op("ve", "tensor_scalar_mul", stat_q[:, :], stat_q[:, :], 1.0 / HID)
    op("ve", "tensor_mul", hscr1[:, :], stat_m[:, :], stat_m[:, :])
    op("ve", "tensor_sub", stat_q[:, :], stat_q[:, :], hscr1[:, :])
    op("ve", "tensor_scalar_add", stat_q[:, :], stat_q[:, :], EPS)
    emit_heron("ve", rstd1[:, :], stat_q[:, :], hscr1[:, :], hscr2[:, :])
    op("ve", "tensor_mul", bias1[:, :], stat_m[:, :], rstd1[:, :])
    op("ve", "tensor_scalar_mul", bias1[:, :], bias1[:, :], -1.0)
    v_rstd1 = bump("ve")

    wait("ac", "sV", v_rstd1)
    for i, (t0, tn) in enumerate(tb_ranges):
        buf = xtile[i % 2]
        guard("sy", f"xtile{i % 2}")
        w = dma("sy", buf[:, :], xh[t0:t0 + tn, :])
        wait("ac", "sS", w)
        guard("ac", "xln_tm")
        if KLN_ID:
            op("ac", "activation", xln_tm[:, :], buf[:, :], AF.Identity, bias=0.0, scale=1.0)
        else:
            op("ac", "activation", xln_tm[:, :], buf[:, :], AF.Identity,
               bias=bias1[:, i:i + 1], scale=rstd1[:, i:i + 1])
        va = bump("ac")
        used_at(f"xtile{i % 2}", "sA", va)
        wait("te", "sA", va)
        wait("te", "sV", v_const_ve)
        for fb in range(NFB):
            pt, pname = (p_trb, "p_trb") if fb % 2 == 0 else (p_trb2, "p_trb2")
            guard("te", pname)
            op("te", "transpose", pt[:, :], xln_tm[:, fb * 128:(fb + 1) * 128], identb[:, :])
            vt = bump("te")
            wait("ve", "sT", vt)
            op("ve", "tensor_copy", xlnT[:, fb * TH + t0: fb * TH + t0 + tn], pt[:, :])
            vv = bump("ve")
            used_at(pname, "sV", vv)
        used_at("xln_tm", "sT", cT.v)
    if TAPS:
        wait("sy", "sV", cV.v)
        dma("sy", taps["t_xlnT"], xlnT[:, :])
        dma("sy", taps["t_st"][:, 0:17], stat_m[:, :])
        dma("sy", taps["t_st"][:, 17:34], stat_q[:, :])
        dma("sy", taps["t_st"][:, 34:51], rstd1[:, :])
        dma("sy", taps["t_st"][:, 51:68], bias1[:, :])
    if KSTOP == "1":
        return finish(nc, Q, sem, cS, ctr, cc_in, cc_out, ctx)
    sbclose("xln_tm", "xtile1", "xtile0")
    phase_gate()

    w_diffL = dma("sy", diffL[:, :], diffL_d)

    NSTRIP = 2
    wbuf = [sb(f"wbuf{i}", [128, NSTRIP * NFB * 128], BF16) for i in range(2)]
    xbc_slab = sb("xbc_slab", [128, TH + 1], BF16)
    vbuf = sb("vbuf", [128, 512], BF16)
    tnbuf = sb("tnbuf", [128, 512], BF16)
    qbuf = sb("qbuf", [128, 512], BF16)
    xchb = sb("xchb", [128, T], BF16)
    diag_t = [sb(f"diag{k}", [128, 128], BF16) for k in range(KC)]
    xstage = sb("xstage", [128, 4 * 128], BF16)
    sg_chb = sb("sg_chb", [128, T], BF16)

    p_pr = [psopen(f"p_pr{i}", [128, 512]) for i in range(2)]

    slices_all = [(0, 3, True), (3, 512, False), (515, 512, False), (1027, 512, False), (1539, 512, False)]

    def wtile(si, fb, j, nchb):
        return wbuf[si % 2][:, (fb * nchb + j) * 128:(fb * nchb + j) * 128 + 128]

    def emit_proj_chb(si, j, nchb, dst_slab, is_dt=False, skip_halo=False, dst_off=0):
        for snum, (c0, cn, halo_only) in enumerate(slices_all):
            if halo_only and skip_halo:
                continue
            par = snum % 2
            pname = f"p_pr{par}"
            guard("te", pname)
            rows = 64 if is_dt else 128
            for fb in range(NFB):
                lh = (wbuf[0][:, fb * 64: fb * 64 + 64] if is_dt
                      else wtile(si, fb, j, nchb))
                op("te", "matmul", p_pr[par][0:rows, 0:cn], lh,
                   xlnT[:, fb * TH + c0: fb * TH + c0 + cn],
                   start=(fb == 0), stop=(fb == NFB - 1))
            vt = bump("te")
            if is_dt:
                wait("ac", "sT", vt)
                op("ac", "activation", dst_slab[:, c0 - dst_off: c0 - dst_off + cn],
                   p_pr[par][0:rows, 0:cn], AF.Identity, bias=dtbias_s[:, 0:1], scale=1.0)
                va = bump("ac")
                used_at(pname, "sA", va)
            else:
                wait("ve", "sT", vt)
                op("ve", "tensor_copy", dst_slab[:, c0 - dst_off: c0 - dst_off + cn],
                   p_pr[par][0:rows, 0:cn])
                vv = bump("ve")
                used_at(pname, "sV", vv)

    def emit_silu(psum_ap, pname, b_col, bh_col, dst_ap, dn):
        guard("ac", "vbuf")
        if b_col is None:
            op("ac", "activation", vbuf[:, 0:dn], psum_ap, AF.Copy)
            op("ac", "activation", tnbuf[:, 0:dn], psum_ap, AF.Tanh, scale=0.5)
        else:
            op("ac", "activation", vbuf[:, 0:dn], psum_ap, AF.Identity, bias=b_col, scale=1.0)
            op("ac", "activation", tnbuf[:, 0:dn], psum_ap, AF.Tanh, bias=bh_col, scale=0.5)
        va = bump("ac")
        used_at(pname, "sA", va)
        wait("ve", "sA", va)
        op("ve", "tensor_scalar", qbuf[:, 0:dn], tnbuf[:, 0:dn], 0.5, 0.5, ALU.mult, ALU.add)
        op("ve", "tensor_mul", dst_ap, qbuf[:, 0:dn], vbuf[:, 0:dn])
        vv = bump("ve")
        used_at("vbuf", "sV", vv)
        return vv

    xbc_strips = [(s, min(NSTRIP, NCHB_BC - s)) for s in range(0, NCHB_BC, NSTRIP)]
    if KCHB1:
        xbc_strips = [(0, 1)]
    strip_w = []

    def load_xbc_strip(si):
        if si < len(xbc_strips):
            chb0, nchb = xbc_strips[si]
            guard("gp", f"wbuf{si % 2}")
            src = W_in[:, 4096 + chb0 * 128: 4096 + (chb0 + nchb) * 128].rearrange("(fb p) c -> p fb c", p=128)
            strip_w.append(dma("gp", wbuf[si % 2][:, 0:NFB * nchb * 128], src))

    for si in range(2):
        load_xbc_strip(si)

    for si, (chb0, nchb) in enumerate(xbc_strips):
        wait("te", "sG", strip_w[si])
        for j in range(nchb):
            chb = chb0 + j
            guard("ve", "xbc_slab")
            wait("te", "sV", cV.v)
            wait("te", "sA", cA.v)
            wait("te", "sS", cS.v)
            wait("te", "sG", cG.v)
            emit_proj_chb(si, j, nchb, xbc_slab)
            wait("ve", "sS", v_const)
            for k in range(KC):
                guard("ve", f"diag{k}")
                op("ve", "tensor_tensor", diag_t[k][:, :], identf[:, :],
                   wconv_s[:, chb * KC + k: chb * KC + k + 1].broadcast_to([128, 128]), ALU.mult)
            v_diag = bump("ve")
            wait("te", "sV", v_diag)
            is_b = chb == 32
            is_c = chb == 33
            dst = B_FM if is_b else C_FM if is_c else xchb
            if not (is_b or is_c):
                guard("ve", "xchb")
            for csl in range(4):
                par = csl % 2
                pname = f"p_pr{par}"
                guard("te", pname)
                for k in range(KC):
                    op("te", "matmul", p_pr[par][:, :], diag_t[k][:, :],
                       xbc_slab[:, csl * 512 + k: csl * 512 + k + 512],
                       start=(k == 0), stop=(k == KC - 1))
                vt = bump("te")
                wait("ac", "sT", vt)
                emit_silu(p_pr[par][:, :], pname, convb_s[:, chb:chb + 1], convbh_s[:, chb:chb + 1],
                          dst[:, csl * 512: csl * 512 + 512], 512)
            used_at("xbc_slab", "sT", cT.v)
            for k in range(KC):
                used_at(f"diag{k}", "sT", cT.v)

            if not (is_b or is_c):
                wait("te", "sV", cV.v)
                for cg in range(4):
                    for cc in range(4):
                        c = cg * 4 + cc
                        pt, pname = (p_trb, "p_trb") if cc % 2 == 0 else (p_trb2, "p_trb2")
                        guard("te", pname)
                        op("te", "transpose", pt[:, :], xchb[:, c * 128:(c + 1) * 128], identb[:, :])
                        vt = bump("te")
                        wait("ve", "sT", vt)
                        guard("ve", "xstage")
                        op("ve", "tensor_copy", xstage[:, cc * 128:(cc + 1) * 128], pt[:, :])
                        vv = bump("ve")
                        used_at(pname, "sV", vv)
                    wait("sy", "sV", cV.v)
                    w = dma("sy", sp_xT[cg * 4:(cg + 1) * 4, :, chb * 128:(chb + 1) * 128].transpose([1, 0, 2]),
                            xstage[:, :])
                    used_at("xstage", "sS", w)
                used_at("xchb", "sT", cT.v)
        used_at(f"wbuf{si % 2}", "sT", cT.v)
        load_xbc_strip(si + 2)

    if TAPS:
        wait("sy", "sV", cV.v)
        dma("sy", taps["t_B"], B_FM[:, :])
        dma("sy", taps["t_C"], C_FM[:, :])

    if KSTOP == "2a":
        return finish(nc, Q, sem, cS, ctr, cc_in, cc_out, ctx)
    if TAPS:
        wait("sy", "sV", cV.v)
        dma("sy", taps["t_misc"][:, 32:80], dbg[:, 32:80])
    guard("gp", "wbuf0")
    wdt = dma("gp", wbuf[0][:, 0:NFB * 64],
              W_in[:, 8448:8512].rearrange("(fb p) c -> p fb c", p=128))
    wait("ve", "sS", v_const)
    op("ve", "memset", dtv[:, :], 0.0)
    v_dtv0 = bump("ve")
    wait("te", "sG", wdt)
    wait("ac", "sV", v_dtv0)
    emit_proj_chb(0, 0, 1, dtv, is_dt=True, skip_halo=True)
    used_at("wbuf0", "sT", cT.v)

    spw = sb("spw", [NH, T]); spy = sb("spy", [NH, T]); spe = sb("spe", [NH, T])
    op("ac", "activation", spe[:, :], dtv[:, 3:TH], AF.Abs)
    op("ac", "activation", spe[:, :], spe[:, :], AF.Exp, scale=-1.0)
    va = bump("ac")
    wait("ve", "sA", va)
    op("ve", "tensor_scalar_add", spw[:, :], spe[:, :], 1.0)
    op("ve", "tensor_scalar_mul", spy[:, :], spe[:, :], 0.7)
    vv = bump("ve")
    for _ in range(4):
        wait("ac", "sV", vv)
        op("ac", "activation", spe[:, :], spy[:, :], AF.Exp, scale=-1.0)
        va = bump("ac")
        wait("ve", "sA", va)
        op("ve", "tensor_mul", spe[:, :], spe[:, :], spw[:, :])
        op("ve", "tensor_scalar_add", spy[:, :], spy[:, :], -1.0)
        op("ve", "tensor_add", spy[:, :], spy[:, :], spe[:, :])
        vv = bump("ve")
    wait("ac", "sV", vv)
    op("ac", "activation", spe[:, :], dtv[:, 3:TH], AF.Relu)
    va = bump("ac")
    wait("ve", "sA", va)
    op("ve", "tensor_add", dtv[:, 3:TH], spy[:, :], spe[:, :])
    v_dt = bump("ve")
    if TAPS:
        wait("sy", "sV", v_dt)
        dma("sy", taps["t_dt"], dtv[:, :])
    sbclose("spe", "spy", "spw")
    phase_gate()

    wait("ve", "sS", w_diffL)
    wait("ve", "sV", v_dt)
    for c in range(NCH):
        op("ve", "tensor_scalar", dAbuf[:, :], dtv[:, 3 + c * L: 3 + (c + 1) * L],
           Acol_s[:, 0:1], None, ALU.mult)
        op("ve", "tensor_tensor_scan", diffL[0:64, c * L:(c + 1) * L],
           ones64f[:, :], dAbuf[:, :], 0.0, ALU.mult, ALU.add)
    v_cA = bump("ve")
    wait("ac", "sV", v_cA)
    for c in range(NCH):
        op("ac", "activation", dcols[:, c:c + 1], diffL[0:64, c * L + L - 1: c * L + L], AF.Exp)
    v_d = bump("ac")
    wait("ve", "sA", v_d)
    op("ve", "memset", Dprod[:, 0:1], 1.0)
    for c in range(NCH):
        op("ve", "tensor_mul", Dprod[:, c + 1:c + 2], Dprod[:, c:c + 1], dcols[:, c:c + 1])
    v_dp = bump("ve")

    if KSTOP == "2adt":
        return finish(nc, Q, sem, cS, ctr, cc_in, cc_out, ctx)
    carry = sb("carry", [128, INTER])
    xw = sb("xw", [128, INTER], BF16)

    psclose("p_pr1", "p_pr0")
    phase_gate()
    p_st = [psopen(f"p_st{i}", [128, 512]) for i in range(2)]
    p_gate = [psopen(f"p_gate{i}", [128, 512]) for i in range(2)]

    op("ve", "memset", carry[:, :], 0.0)
    bump("ve")
    wait("te", "sA", cA.v)
    wait("te", "sV", cV.v)

    xT_ld = []

    def load_xT(c, tag):
        if c < NCH:
            guard("sy", f"xTb{c % 2}@{tag}")
            xT_ld.append(dma("sy", xTb[c % 2][:, :], sp_xT[c, :, :]))
        else:
            xT_ld.append(0)

    for c in range(2):
        load_xT(c, "p2b")

    for c in range(NCH):
        wait("ac", "sV", v_cA)
        guard("ac", "decayf")
        op("ac", "activation", decayf[:, :], diffL[0:64, c * L:(c + 1) * L], AF.Exp,
           bias=diffL[0:64, c * L + L - 1: c * L + L], scale=-1.0)
        va = bump("ac")
        wait("ve", "sA", va)
        guard("ve", "wstf")
        op("ve", "tensor_mul", wstf[:, :], decayf[:, :], dtv[:, 3 + c * L: 3 + (c + 1) * L])
        vv = bump("ve")
        used_at("decayf", "sV", vv)
        wait("te", "sV", vv)
        guard("te", "p_trf")
        op("te", "transpose", p_trf[:, 0:64], wstf[:, :], identf[0:64, 0:64])
        vt = bump("te")
        used_at("wstf", "sT", vt)
        wait("ve", "sT", vt)
        guard("ve", "wstTM")
        op("ve", "tensor_copy", wstTM[:, :], p_trf[:, 0:64])
        vv1 = bump("ve")
        used_at("p_trf", "sV", vv1)
        guard("te", "p_trf")
        op("te", "transpose", p_trf[:, 0:64], dtv[:, 3 + c * L: 3 + (c + 1) * L], identf[0:64, 0:64])
        vt = bump("te")
        wait("ve", "sT", vt)
        op("ve", "tensor_copy", dtTM[:, c * NH:(c + 1) * NH], p_trf[:, 0:64])
        vv2 = bump("ve")
        used_at("p_trf", "sV", vv2)
        guard("te", "p_trf")
        op("te", "transpose", p_trf[:, 0:64], diffL[0:64, c * L:(c + 1) * L], identf[0:64, 0:64])
        vt = bump("te")
        wait("ac", "sT", vt)
        op("ac", "activation", uall[:, c * NH:(c + 1) * NH], p_trf[:, 0:64], AF.Exp)
        va = bump("ac")
        used_at("p_trf", "sA", va)
        guard("te", "p_trb")
        op("te", "transpose", p_trb[:, :], B_FM[:, c * L:(c + 1) * L], identb[:, :])
        vt = bump("te")
        wait("ve", "sT", vt)
        guard("ve", "BtTM")
        op("ve", "tensor_copy", BtTM[:, :], p_trb[:, :])
        v_bt = bump("ve")
        used_at("p_trb", "sV", v_bt)

        guard("ve", "diagtmp")
        op("ve", "tensor_tensor", diagtmp[:, :], identf[0:64, 0:64],
           diffL[0:64, c * L + L - 1: c * L + L].broadcast_to([64, 64]), ALU.mult)
        v_dg = bump("ve")
        wait("te", "sV", v_dg)
        guard("te", "p_ms")
        op("te", "matmul", p_ms[:, 0:64], ones64f[:, :], diagtmp[:, :], start=True, stop=True)
        vt = bump("te")
        used_at("diagtmp", "sT", vt)
        wait("ac", "sT", vt)
        guard("ac", "dbc")
        op("ac", "activation", dbc[:, :], p_ms[:, 0:64], AF.Exp)
        v_dbc = bump("ac")
        used_at("p_ms", "sA", v_dbc)

        wait("gp", "sS", xT_ld[c])
        wait("gp", "sV", vv1)
        guard("gp", "xw")
        op("gp", "tensor_tensor", xw[:, :], xTb[c % 2][:, :],
           wstTM[:, :].unsqueeze(2).broadcast_to([128, NH, HD]), ALU.mult)
        v_xw = bump("gp")
        used_at("wstTM", "sP", v_xw)
        used_at(f"xTb{c % 2}@p2b", "sP", v_xw)

        wait("sy", "sV", cV.v)
        w_cs = dma("sy", sp_carry[c, :, :], carry[:, :])

        wait("gp", "sA", v_dbc)
        wait("gp", "sS", w_cs)
        wait("gp", "sV", cV.v)
        op("gp", "tensor_tensor", carry[:, :], carry[:, :],
           dbc[:, 0:NH].unsqueeze(2).broadcast_to([128, NH, HD]), ALU.mult)
        v_ct = bump("gp")
        used_at("dbc", "sP", v_ct)
        wait("te", "sV", v_bt)
        wait("te", "sP", v_xw)
        for sl in range(8):
            par = sl % 2
            pname = f"p_st{par}"
            guard("te", pname)
            op("te", "matmul", p_st[par][:, :], BtTM[:, :], xw[:, sl * 512:(sl + 1) * 512],
               start=True, stop=True)
            vt = bump("te")
            wait("ve", "sT", vt)
            wait("ve", "sP", v_ct)
            op("ve", "tensor_add", carry[:, sl * 512:(sl + 1) * 512],
               carry[:, sl * 512:(sl + 1) * 512], p_st[par][:, :])
            vv = bump("ve")
            used_at(pname, "sV", vv)
        used_at("BtTM", "sT", cT.v)
        used_at("xw", "sT", cT.v)
        load_xT(c + 2, "p2b")

    v_cfin = cV.v
    wait("sy", "sV", v_cfin)
    w_cc1 = dma("sy", cc_in[0:128, :], carry[:, :])
    if TAPS:
        dma("sy", taps["t_carry"], carry[:, :])
        dma("sy", taps["t_xw"], xw[:, :])
        dma("sy", taps["t_dbc"], dbc[:, :])
        dma("sy", taps["t_xbc"], sp_xT[15, :, :])
    wait("te", "sV", v_dp)
    guard("te", "p_trf")
    op("te", "transpose", p_trf[0:1, 0:64], Dprod[:, NCH:NCH + 1], identf[0:64, 0:64])
    vt = bump("te")
    wait("ve", "sT", vt)
    op("ve", "tensor_copy", prow[0:1, 0:64], p_trf[0:1, 0:64])
    v_pc = bump("ve")
    used_at("p_trf", "sV", v_pc)
    wait("sy", "sV", v_pc)
    w_cc2 = dma("sy", cc_in[128:129, 0:64], prow[0:1, 0:64])
    wait("gp", "sS", max(w_cc1, w_cc2))
    Q["gp"].append(["cc", []])
    cC.v += 0
    if TAPS:
        wait("sy", "sC", cC.v + 1)
        dma("sy", taps["t_cin"], cc_in[:, :])
    cC.v += 1
    v_cc = cC.v
    sbclose("xw", "carry")
    phase_gate()

    if KSTOP == "2b":
        return finish(nc, Q, sem, cS, ctr, cc_in, cc_out, ctx)
    gate_strips = [(s, NSTRIP) for s in range(0, NGATE, NSTRIP)]
    gstrip_w = []

    def load_gate_strip(si):
        if si < len(gate_strips):
            chb0, nchb = gate_strips[si]
            guard("gp", f"wbuf{si % 2}")
            src = W_in[:, chb0 * 128: (chb0 + nchb) * 128].rearrange("(fb p) c -> p fb c", p=128)
            gstrip_w.append(dma("gp", wbuf[si % 2][:, 0:NFB * nchb * 128], src))

    for si in range(2):
        load_gate_strip(si)
    for si, (chb0, nchb) in enumerate(gate_strips):
        wait("te", "sG", gstrip_w[si])
        for j in range(nchb):
            chb = chb0 + j
            guard("ve", "sg_chb")
            for snum, (c0, cn, halo_only) in enumerate(slices_all):
                if halo_only:
                    continue
                par = snum % 2
                pname = f"p_gate{par}"
                guard("te", pname)
                for fb in range(NFB):
                    op("te", "matmul", p_gate[par][:, 0:cn], wtile(si, fb, j, nchb),
                       xlnT[:, fb * TH + c0: fb * TH + c0 + cn],
                       start=(fb == 0), stop=(fb == NFB - 1))
                vt = bump("te")
                wait("ac", "sT", vt)
                emit_silu(p_gate[par][:, 0:cn], pname, None, None,
                          sg_chb[:, c0 - 3: c0 - 3 + cn], cn)
            wait("sy", "sV", cV.v)
            w = dma("sy", sp_sg[chb, :, :], sg_chb[:, :])
            used_at("sg_chb", "sS", w)
        used_at(f"wbuf{si % 2}", "sT", cT.v)
        load_gate_strip(si + 2)

    sbclose("sg_chb", "xstage", "diag3", "diag2", "diag1", "diag0", "xchb",
            "qbuf", "tnbuf", "vbuf", "xbc_slab", "wbuf1", "wbuf0")
    sbclose("xlnT")
    phase_gate()

    if KSTOP == "2c":
        return finish(nc, Q, sem, cS, ctr, cc_in, cc_out, ctx)
    tpeer = [sb(f"tpeer{i}", [128, INTER]) for i in range(2)]
    gtmp = sb("gtmp", [128, INTER])

    op("ve", "memset", globalC[:, :], 0.0)
    v_g = bump("ve")
    wait("sy", "sC", v_cc)
    wpc = 0
    for k in (1, 2):
        wpc = dma("sy", pcol[:, k:k + 1], cc_out[129 * k + 128:129 * k + 129, 0:64].transpose([1, 0]))
    tld = []
    for jj in range(2):
        tld.append(dma("sy", tpeer[jj % 2][:, :], cc_out[129 * jj: 129 * jj + 128, :]))
    wait("ve", "sS", wpc)
    for k in (1, 2):
        op("ve", "tensor_scalar_add", pcol[:, k:k + 1], pcol[:, k:k + 1], -1.0)
        op("ve", "tensor_scalar", pcol[:, k:k + 1], pcol[:, k:k + 1],
           cmask_s[:, 3 + (k - 1):3 + k], 1.0, ALU.mult, ALU.add)
    op("ve", "tensor_scalar", coefc[:, 2:3], ones64f[:, 0:1], cmask_s[:, 2:3], None, ALU.mult)
    op("ve", "tensor_scalar", coefc[:, 1:2], pcol[:, 2:3], cmask_s[:, 1:2], None, ALU.mult)
    op("ve", "tensor_mul", coefc[:, 0:1], pcol[:, 1:2], pcol[:, 2:3])
    op("ve", "tensor_scalar", coefc[:, 0:1], coefc[:, 0:1], cmask_s[:, 0:1], None, ALU.mult)
    v_coef = bump("ve")
    for jj in range(3):
        wait("ve", "sV", v_coef)
        guard("ve", "diagtmp")
        op("ve", "tensor_tensor", diagtmp[:, :], identf[0:64, 0:64],
           coefc[:, jj:jj + 1].broadcast_to([64, 64]), ALU.mult)
        v_dg = bump("ve")
        wait("te", "sV", v_dg)
        guard("te", "p_ms")
        op("te", "matmul", p_ms[:, 0:64], ones64f[:, :], diagtmp[:, :], start=True, stop=True)
        vt = bump("te")
        used_at("diagtmp", "sT", vt)
        wait("ac", "sT", vt)
        guard("ac", "cbc")
        op("ac", "activation", cbc[:, :], p_ms[:, 0:64], AF.Copy)
        va = bump("ac")
        used_at("p_ms", "sA", va)
        wait("gp", "sA", va)
        wait("gp", "sS", tld[jj])
        guard("gp", "gtmp")
        op("gp", "tensor_tensor", gtmp[:, :], tpeer[jj % 2][:, :],
           cbc[:, 0:NH].unsqueeze(2).broadcast_to([128, NH, HD]), ALU.mult)
        vg = bump("gp")
        used_at("cbc", "sP", vg)
        used_at(f"tpeer{jj % 2}", "sP", vg)
        wait("ve", "sP", vg)
        op("ve", "tensor_add", globalC[:, :], globalC[:, :], gtmp[:, :])
        v_g = bump("ve")
        used_at("gtmp", "sV", v_g)
        if jj + 2 < 3:
            guard("sy", f"tpeer{(jj + 2) % 2}")
            tld.append(dma("sy", tpeer[(jj + 2) % 2][:, :], cc_out[129 * (jj + 2): 129 * (jj + 2) + 128, :]))
    v_glob = v_g
    if TAPS:
        wait("sy", "sV", v_glob)
        dma("sy", taps["t_glob"], globalC[:, :])
    sbclose("gtmp", "tpeer1", "tpeer0")
    phase_gate()

    if KSTOP == "3":
        return finish(nc, Q, sem, cS, ctr, cc_in, cc_out, ctx)
    psclose("p_gate1", "p_gate0", "p_st1", "p_st0", "p_trf")
    phase_gate()
    p_df = psopen("p_df", [128, 512])
    p_ia = [psopen(f"p_ia{i}", [128, 512]) for i in range(2)]
    p_ie = [psopen(f"p_ie{i}", [128, 512]) for i in range(2)]

    diffR = sb("diffR", [97, NH * L])
    w_diffR = dma("sy", diffR[:, :], diffR_d)
    Ebuf = sb("Ebuf", [128, NH * L], BF16)
    z2buf = Ebuf[:, 0:NCHB_X * 128]
    Mt = sb("Mt", [128, NH * L], BF16)
    Gt_m = sb("Gt_m", [128, 128], BF16)
    Dpbc = sb("Dpbc", [128, NH], BF16)
    Gsc = sb("Gsc", [128, INTER], BF16)
    xdt = sb("xdt", [128, INTER], BF16)
    xDb = sb("xDb", [128, INTER], BF16)
    cbb = [sb(f"cbb{i}", [128, INTER], BF16) for i in range(2)]
    sgb = [sb(f"sgb{i}", [128, NCHB_X * 128], BF16) for i in range(2)]
    ybuf = sb("ybuf", [128, INTER], BF16)
    zbuf = Mt[:, 0:NCHB_X * 128]
    ev_t1 = sb("ev_t1", [128, 512])

    xT2, cb2, sg2 = [], [], []

    def load_p4(c):
        if c < NCH:
            guard("sy", f"xTb{c % 2}@p4")
            xT2.append(dma("sy", xTb[c % 2][:, :], sp_xT[c, :, :]))
            guard("gp", f"cbb{c % 2}")
            cb2.append(dma("gp", cbb[c % 2][:, :], sp_carry[c, :, :]))
            guard("sy", f"sgb{c % 2}")
            sg2.append(dma("sy", sgb[c % 2][:, :], sp_sg[:, :, c * L:(c + 1) * L].transpose([1, 0, 2])))
        else:
            xT2.append(0); cb2.append(0); sg2.append(0)

    for c in range(2):
        load_p4(c)
    w_p4pre = cS.v

    for c in range(NCH):
        wait("gp", "sS", w_p4pre)
        wait("gp", "sV", v_cA)
        guard("gp", "diffR64")
        w_fl = dma("gp", diffR[64:65, :], diffL[0:64, c * L:(c + 1) * L])
        guard("te", "p_ms")
        op("te", "matmul", p_ms[:, :], B_FM[:, c * L:(c + 1) * L], C_FM[:, c * L:(c + 1) * L],
           start=True, stop=True)
        vt = bump("te")
        wait("ve", "sT", vt)
        guard("ve", "Gt_m")
        op("ve", "tensor_tensor", Gt_m[:, :], p_ms[:, :], triu_f[:, :], ALU.mult)
        v_gt = bump("ve")
        used_at("p_ms", "sV", v_gt)
        wait("ve", "sV", v_dp)
        guard("ve", "diagtmp")
        op("ve", "tensor_tensor", diagtmp[:, :], identf[0:64, 0:64],
           Dprod[:, c:c + 1].broadcast_to([64, 64]), ALU.mult)
        v_dg = bump("ve")
        wait("te", "sV", v_dg)
        guard("te", "p_ms")
        op("te", "matmul", p_ms[:, 0:64], ones64f[:, :], diagtmp[:, :], start=True, stop=True)
        vt = bump("te")
        used_at("diagtmp", "sT", vt)
        wait("ve", "sT", vt)
        guard("ve", "Dpbc")
        op("ve", "tensor_copy", Dpbc[:, :], p_ms[:, 0:64])
        v_dpb = bump("ve")
        used_at("p_ms", "sV", v_dpb)
        wait("gp", "sV", v_dpb)
        wait("gp", "sV", v_glob)
        guard("gp", "Gsc")
        op("gp", "tensor_tensor", Gsc[:, :], globalC[:, :],
           Dpbc[:, 0:NH].unsqueeze(2).broadcast_to([128, NH, HD]), ALU.mult)
        v_gsc = bump("gp")
        used_at("Dpbc", "sP", v_gsc)

        wait("gp", "sS", xT2[c])
        guard("gp", "xdt")
        op("gp", "tensor_tensor", xdt[:, :], xTb[c % 2][:, :],
           dtTM[:, c * NH:(c + 1) * NH].unsqueeze(2).broadcast_to([128, NH, HD]), ALU.mult)
        guard("gp", "xDb")
        op("gp", "tensor_tensor", xDb[:, :], xTb[c % 2][:, :],
           Dbc_s[:, 0:NH].unsqueeze(2).broadcast_to([128, NH, HD]), ALU.mult)
        v_xops = bump("gp")
        used_at(f"xTb{c % 2}@p4", "sP", v_xops)
        if TAPS and c == 5:
            wait("sy", "sP", v_xops)
            dma("sy", taps["t_xbc"], xTb[c % 2][:, :])

        guard("ac", "Ebuf")
        for dsl in range(NFB):
            wait("te", "sG", w_fl)
            wait("te", "sS", w_p4pre)
            wait("te", "sV", v_cA)
            guard("te", "p_df")
            for half in range(2):
                op("te", "matmul", p_df[:, half * 256:(half + 1) * 256],
                   diffL[:, c * L:(c + 1) * L],
                   diffR[:, dsl * 512 + half * 256: dsl * 512 + (half + 1) * 256],
                   start=True, stop=True)
            vt = bump("te")
            if dsl == NFB - 1:
                used_at("diffR64", "sT", vt)
            wait("ac", "sT", vt)
            op("ac", "activation", Ebuf[:, dsl * 512:(dsl + 1) * 512], p_df[:, :], AF.Exp)
            va = bump("ac")
            used_at("p_df", "sA", va)
        v_E = cA.v
        wait("ve", "sA", v_E)
        if TAPS and c == 0:
            wait("sy", "sG", w_fl)
            wait("sy", "sS", w_diffR)
            wait("sy", "sA", v_E)
            wsn = dma("sy", taps["t_misc"][64:65, 0:16], diffR[64:65, 0:16])
            wsn = dma("sy", taps["t_misc"][0:1, 16:32], diffL[0:1, 0:16])
            used_also("diffR64", "sS", wsn)
        wait("ve", "sV", v_gt)
        guard("ve", "MtL")
        op("ve", "tensor_tensor", Mt[:, 0:NH * L // 2], Ebuf[:, 0:NH * L // 2],
           Gt_m[:, :].unsqueeze(1).broadcast_to([128, NH // 2, 128]), ALU.mult)
        v_mt1 = bump("ve")
        wait("gp", "sA", v_E)
        wait("gp", "sV", v_gt)
        guard("gp", "MtH")
        op("gp", "tensor_tensor", Mt[:, NH * L // 2:], Ebuf[:, NH * L // 2:],
           Gt_m[:, :].unsqueeze(1).broadcast_to([128, NH // 2, 128]), ALU.mult)
        v_mt2 = bump("gp")
        used_at("Gt_m", "sP", v_mt2)

        wait("te", "sP", v_gsc)
        wait("te", "sP", v_xops)
        wait("te", "sP", v_mt2)
        wait("te", "sV", v_mt1)
        wait("te", "sG", cb2[c])
        for q8 in range(8):
            par = q8 % 2
            pa, pe_ = f"p_ia{par}", f"p_ie{par}"
            guard("te", pa)
            guard("te", pe_)
            op("te", "matmul", p_ia[par][:, :], identb[:, :], xDb[:, q8 * 512:(q8 + 1) * 512],
               start=True, stop=False)
            for hh in range(8):
                h = q8 * 8 + hh
                op("te", "matmul", p_ia[par][:, hh * 64:(hh + 1) * 64], Mt[:, h * 128:(h + 1) * 128],
                   xdt[:, h * 64:(h + 1) * 64], start=False, stop=(hh == 7))
            op("te", "matmul", p_ie[par][:, :], C_FM[:, c * L:(c + 1) * L],
               cbb[c % 2][:, q8 * 512:(q8 + 1) * 512], start=True, stop=False)
            op("te", "matmul", p_ie[par][:, :], C_FM[:, c * L:(c + 1) * L],
               Gsc[:, q8 * 512:(q8 + 1) * 512], start=False, stop=True)
            vt = bump("te")
            wait("ve", "sT", vt)
            guard("ve", "ev_t1")
            op("ve", "tensor_tensor", ev_t1[:, :], p_ie[par][:, :],
               uall[:, c * NH + q8 * 8: c * NH + (q8 + 1) * 8].unsqueeze(2).broadcast_to([128, 8, 64]),
               ALU.mult)
            if q8 == 0:
                guard("ve", "ybuf")
            op("ve", "tensor_add", ybuf[:, q8 * 512:(q8 + 1) * 512], ev_t1[:, :], p_ia[par][:, :])
            vv = bump("ve")
            used_at(pa, "sV", vv)
            used_at(pe_, "sV", vv)
        v_y = cV.v
        v_te_y = cT.v
        used_at("MtL", "sT", v_te_y)
        used_at("MtH", "sT", v_te_y)
        used_at("xdt", "sT", v_te_y)
        used_at("xDb", "sT", v_te_y)
        used_at("Gsc", "sT", v_te_y)
        used_at(f"cbb{c % 2}", "sT", v_te_y)
        if TAPS and c == 0:
            wait("ve", "sG", cb2[0])
            wait("ve", "sP", v_mt2)
            op("ve", "tensor_copy", dbg[:, 32:48], Mt[:, 0:16])
            op("ve", "tensor_copy", dbg[:, 48:64], Ebuf[:, 0:16])
            op("ve", "tensor_copy", dbg[:, 64:72], uall[:, 0:8])
            op("ve", "tensor_copy", dbg[:, 72:80], xDb[:, 0:8])
            vv = bump("ve")
            for _nm in ("xDb", "MtL", "Ebuf"):
                used_also(_nm, "sV", vv)
        if TAPS and c == 5:
            wait("sy", "sV", v_y)
            dma("sy", taps["t_y"], ybuf[:, :])

        wait("te", "sV", v_y)
        for chb in range(NCHB_X):
            pt, pname = (p_trb, "p_trb") if chb % 2 == 0 else (p_trb2, "p_trb2")
            guard("te", pname)
            op("te", "transpose", pt[:, :], ybuf[:, chb * 128:(chb + 1) * 128], identb[:, :])
            vt = bump("te")
            wait("ve", "sT", vt)
            wait("ve", "sS", sg2[c])
            if chb == 0:
                guard("ve", "MtL")
            op("ve", "tensor_tensor", zbuf[:, chb * 128:(chb + 1) * 128], pt[:, :],
               sgb[c % 2][:, chb * 128:(chb + 1) * 128], ALU.mult)
            vv = bump("ve")
            used_at(pname, "sV", vv)
        used_at("ybuf", "sT", cT.v)
        used_at(f"sgb{c % 2}", "sV", cV.v)
        guard("ve", "Ebuf")
        op("ve", "tensor_mul", z2buf, zbuf[:, :], zbuf[:, :])
        v_z2 = bump("ve")
        wait("te", "sV", v_z2)
        guard("te", "p_ms")
        for chb in range(NCHB_X):
            op("te", "matmul", p_ms[0:1, :], ones128b[:, 0:1], z2buf[:, chb * 128:(chb + 1) * 128],
               start=(chb == 0), stop=(chb == NCHB_X - 1))
        vt = bump("te")
        used_at("Ebuf", "sT", vt)
        wait("ve", "sT", vt)
        guard("ve", "vrow")
        op("ve", "tensor_copy", vrow[0:1, :], p_ms[0:1, :])
        v_vr = bump("ve")
        used_at("p_ms", "sV", v_vr)
        wait("te", "sV", v_vr)
        guard("te", "p_df")
        op("te", "transpose", p_df[:, 0:1], vrow[0:1, :], identf[0:1, 0:1])
        vt = bump("te")
        used_at("vrow", "sT", vt)
        wait("ve", "sT", vt)
        op("ve", "tensor_copy", varz[:, c:c + 1], p_df[:, 0:1])
        vv = bump("ve")
        used_at("p_df", "sV", vv)
        wait("sy", "sV", v_z2)
        w_zs = dma("sy", sp_z[:, :, c * L:(c + 1) * L].transpose([1, 0, 2]), zbuf[:, :])
        used_also("MtL", "sS", w_zs)
        if TAPS and c == 5:
            dma("sy", taps["t_z"], zbuf[:, :])
        load_p4(c + 2)

    hz1 = hscr1; hz2 = hscr2
    op("ve", "tensor_scalar_mul", varz[:, :], varz[:, :], 1.0 / INTER)
    op("ve", "tensor_scalar_add", varz[:, :], varz[:, :], EPS)
    emit_heron("ve", rstdz[:, :], varz[:, :], hz1[:, 0:NCH], hz2[:, 0:NCH])
    v_rz = bump("ve")

    sbclose("ev_t1", "ybuf", "sgb1", "sgb0", "cbb1", "cbb0",
            "xDb", "xdt", "Gsc", "Dpbc", "Gt_m", "Mt", "Ebuf", "diffR")
    sbclose("globalC", "xTb1", "xTb0", "uall", "dtTM", "diffL", "dtv", "C_FM", "B_FM")
    phase_gate()

    if KSTOP == "4":
        return finish(nc, Q, sem, cS, ctr, cc_in, cc_out, ctx)
    psclose("p_ie1", "p_ie0", "p_ia1", "p_ia0", "p_df")
    phase_gate()
    p_o = [psopen(f"p_o{i}", [128, 512]) for i in range(2)]

    wout_sb = sb("wout_sb", [128, NCHB_X * HID], BF16)
    zc = [sb(f"zc{i}", [128, NCHB_X * 128], BF16) for i in range(2)]
    xrow = [sb(f"xrow{i}", [128, HID]) for i in range(2)]
    x2buf = sb("x2buf", [128, HID])
    o1buf = sb("o1buf", [128, 512])
    x2ln = sb("x2ln", [128, HID], BF16)
    lnstage = sb("lnstage", [128, NFB * 128], FP8)

    wait("sy", "sG", wcast["wout"])
    wwo = 0
    for kb in range(NCHB_X):
        wwo = dma("sy", wout_sb[:, kb * HID:(kb + 1) * HID], wout_bf[kb * 128:(kb + 1) * 128, :])
    wait("ve", "sS", wwo)
    for kb in range(NCHB_X):
        op("ve", "tensor_scalar", wout_sb[:, kb * HID:(kb + 1) * HID],
           wout_sb[:, kb * HID:(kb + 1) * HID], normw_s[:, kb:kb + 1], None, ALU.mult)
    v_wo = bump("ve")

    zld, xld = [], []

    def load_p5(c):
        if c < NCH:
            guard("sy", f"zc{c % 2}")
            zld.append(dma("sy", zc[c % 2][:, :], sp_z[:, :, c * L:(c + 1) * L].transpose([1, 0, 2])))
            guard("sy", f"xrow{c % 2}")
            xld.append(dma("sy", xrow[c % 2][:, :], xh[3 + c * L: 3 + (c + 1) * L, :]))
        else:
            zld.append(0); xld.append(0)

    for c in range(2):
        load_p5(c)

    for c in range(NCH):
        wait("te", "sV", v_wo)
        wait("te", "sS", zld[c])
        for jsl in range(4):
            par = jsl % 2
            pname = f"p_o{par}"
            guard("te", pname)
            for kb in range(NCHB_X):
                op("te", "matmul", p_o[par][:, :], zc[c % 2][:, kb * 128:(kb + 1) * 128],
                   wout_sb[:, kb * HID + jsl * 512: kb * HID + (jsl + 1) * 512],
                   start=(kb == 0), stop=(kb == NCHB_X - 1))
            vt = bump("te")
            wait("ac", "sT", vt)
            wait("ac", "sV", v_rz)
            guard("ac", "o1buf")
            op("ac", "activation", o1buf[:, :], p_o[par][:, :], AF.Copy, scale=rstdz[:, c:c + 1])
            va = bump("ac")
            used_at(pname, "sA", va)
            wait("ve", "sA", va)
            wait("ve", "sS", xld[c])
            if jsl == 0:
                guard("ve", "x2buf")
            op("ve", "tensor_add", x2buf[:, jsl * 512:(jsl + 1) * 512], o1buf[:, :],
               xrow[c % 2][:, jsl * 512:(jsl + 1) * 512])
            vv = bump("ve")
            used_at("o1buf", "sV", vv)
        v_x2 = cV.v
        used_at(f"zc{c % 2}", "sT", cT.v)
        used_at(f"xrow{c % 2}", "sV", v_x2)
        if TAPS and c == 5:
            wait("sy", "sV", v_x2)
            dma("sy", taps["t_x2"], x2buf[:, :])
        wait("sy", "sV", v_x2)
        w_x2 = dma("sy", sp_x2[c, :, :], x2buf[:, :])
        op("ve", "reduce_sum", stat2_m[:, c:c + 1], x2buf[:, :], mybir.AxisListType.X)
        vv_r2 = bump("ve")
        wait("ac", "sV", v_x2)
        guard("ac", "sq_scr")
        op("ac", "activation", sq_scr[:, :], x2buf[:, :], AF.Square, accum_out=stat2_q[:, c:c + 1])
        va = bump("ac")
        used_at("x2buf", "sA", va)
        used_also("x2buf", "sV", vv_r2)
        used_also("x2buf", "sS", w_x2)
        load_p5(c + 2)
    v_stats2 = cA.v
    if KSTOP == "5a":
        return finish(nc, Q, sem, cS, ctr, cc_in, cc_out, ctx)

    wait("ve", "sA", v_stats2)
    wait("ve", "sV", cV.v)
    op("ve", "tensor_scalar_mul", stat2_m[:, :], stat2_m[:, :], 1.0 / HID)
    op("ve", "tensor_scalar_mul", stat2_q[:, :], stat2_q[:, :], 1.0 / HID)
    op("ve", "tensor_mul", hz1[:, 0:NCH], stat2_m[:, :], stat2_m[:, :])
    op("ve", "tensor_sub", stat2_q[:, :], stat2_q[:, :], hz1[:, 0:NCH])
    op("ve", "tensor_scalar_add", stat2_q[:, :], stat2_q[:, :], EPS)
    emit_heron("ve", rstd2[:, :], stat2_q[:, :], hz1[:, 0:NCH], hz2[:, 0:NCH])
    op("ve", "tensor_mul", bias2[:, :], stat2_m[:, :], rstd2[:, :])
    op("ve", "tensor_scalar_mul", bias2[:, :], bias2[:, :], -1.0)
    v_r2 = bump("ve")

    if TAPS:
        wait("sy", "sV", cV.v)
        dma("sy", taps["t_misc"][:, 32:80], dbg[:, 32:80])
    if KSTOP == "5s":
        return finish(nc, Q, sem, cS, ctr, cc_in, cc_out, ctx)
    x2ld = []

    def load_p5b(c):
        if c < NCH:
            guard("sy", f"xrow{c % 2}")
            x2ld.append(dma("sy", xrow[c % 2][:, :], sp_x2[c, :, :]))
        else:
            x2ld.append(0)

    for c in range(2):
        load_p5b(c)
    for c in range(NCH):
        wait("ac", "sV", v_r2)
        wait("ac", "sS", x2ld[c])
        guard("ac", "x2ln")
        op("ac", "activation", x2ln[:, :], xrow[c % 2][:, :], AF.Identity,
           bias=bias2[:, c:c + 1], scale=rstd2[:, c:c + 1])
        va = bump("ac")
        used_at(f"xrow{c % 2}", "sA", va)
        wait("te", "sA", va)
        for fb in range(NFB):
            pt, pname = (p_trb, "p_trb") if fb % 2 == 0 else (p_trb2, "p_trb2")
            guard("te", pname)
            op("te", "transpose", pt[:, :], x2ln[:, fb * 128:(fb + 1) * 128], identb[:, :])
            vt = bump("te")
            wait("ve", "sT", vt)
            if fb == 0:
                guard("ve", "lnstage")
            op("ve", "tensor_copy", lnstage[:, fb * 128:(fb + 1) * 128], pt[:, :])
            vv = bump("ve")
            used_at(pname, "sV", vv)
        used_at("x2ln", "sT", cT.v)
        wait("sy", "sV", cV.v)
        w = dma("sy", sp_x2lnT[:, :, c * L:(c + 1) * L].transpose([1, 0, 2]), lnstage[:, :])
        used_at("lnstage", "sS", w)
        load_p5b(c + 2)

    if KSTOP == "5b":
        return finish(nc, Q, sem, cS, ctr, cc_in, cc_out, ctx)
    sbclose("lnstage", "x2ln", "o1buf", "x2buf", "xrow1", "xrow0", "zc1", "zc0", "wout_sb")
    phase_gate()

    if KSTOP == "5":
        return finish(nc, Q, sem, cS, ctr, cc_in, cc_out, ctx)
    psclose("p_o1", "p_o0", "p_ms", "p_trb2", "p_trb")
    phase_gate()
    p_fg = psopen("p_fg", [128, TT1])
    p_ff = psopen("p_ff", [128, TT1])

    x2lt = sb("x2lt", [128, NFB * TT1], FP8)
    act2 = sb("act2", [128, NJB * TT1], FP8)
    wgb = [sb(f"wgb{i}", [128, 2 * NFB * 128], FP8) for i in range(2)]
    w2b = [sb(f"w2b{i}", [128, 4 * 512], FP8) for i in range(2)]
    xgb = sb("xgb", [128, TT1], BF16)
    f1b = sb("f1b", [128, TT1], BF16)
    gsc1 = sb("gsc1", [128, TT1], BF16)
    gsc2 = gsc1
    obuf = sb("obuf", [128, 512])
    fob = sb("fob", [128, 512], BF16)
    x2sl = [sb(f"x2sl{i}", [128, 512]) for i in range(2)]

    for u in range(2):
        guard("sy", "x2lt")
        w_xt = dma("sy", x2lt[:, :], sp_x2lnT[:, :, u * TT1:(u + 1) * TT1].transpose([1, 0, 2]))
        wait("sy", "sG", max(wcast["wg0"], wcast["wg1"]))
        wait("te", "sV", cV.v)
        wait("te", "sA", cA.v)
        wg_ld = []

        def load_wg(jb):
            if jb < NJB:
                guard("sy", f"wgb{jb % 2}")
                w1 = dma("sy", wgb[jb % 2][:, 0:NFB * 128],
                         wg0_bf[:, jb * 128:(jb + 1) * 128].rearrange("(fb p) c -> p fb c", p=128))
                w2_ = dma("sy", wgb[jb % 2][:, NFB * 128:],
                          wg1_bf[:, jb * 128:(jb + 1) * 128].rearrange("(fb p) c -> p fb c", p=128))
                wg_ld.append(max(w1, w2_))
            else:
                wg_ld.append(0)

        if TAPS and u == 0:
            dma("sy", taps["t_p6a"], sp_x2lnT[:, :, 0:64].transpose([1, 0, 2]))
        for jb in range(2):
            load_wg(jb)
        for jb in range(NJB):
            wait("te", "sS", max(wg_ld[jb], w_xt))
            guard("te", "p_fg")
            guard("te", "p_ff")
            for half in range(2):
                for fb in range(NFB):
                    op("te", "matmul", p_fg[:, half * 512:(half + 1) * 512],
                       wgb[jb % 2][:, fb * 128:(fb + 1) * 128],
                       x2lt[:, fb * TT1 + half * 512: fb * TT1 + (half + 1) * 512],
                       start=(fb == 0), stop=(fb == NFB - 1))
            for half in range(2):
                for fb in range(NFB):
                    op("te", "matmul", p_ff[:, half * 512:(half + 1) * 512],
                       wgb[jb % 2][:, (NFB + fb) * 128:(NFB + fb + 1) * 128],
                       x2lt[:, fb * TT1 + half * 512: fb * TT1 + (half + 1) * 512],
                       start=(fb == 0), stop=(fb == NFB - 1))
            vt = bump("te")
            used_at(f"wgb{jb % 2}", "sT", vt)
            wait("ac", "sT", vt)
            guard("ac", "xgb")
            op("ac", "activation", xgb[:, :], p_fg[:, :], AF.Copy)
            va = bump("ac")
            used_at("p_fg", "sA", va)
            wait("ve", "sT", vt)
            guard("ve", "f1b")
            op("ve", "tensor_copy", f1b[:, :], p_ff[:, :])
            v_f1 = bump("ve")
            used_at("p_ff", "sV", v_f1)
            wait("gp", "sA", va)
            guard("gp", "gsc1")
            op("gp", "tensor_mul", gsc1[:, :], xgb[:, :], xgb[:, :])
            op("gp", "tensor_scalar", gsc1[:, :], gsc1[:, :], GELU_C2, 1.0, ALU.mult, ALU.add)
            op("gp", "tensor_mul", gsc1[:, :], gsc1[:, :], xgb[:, :])
            vg = bump("gp")
            wait("ac", "sP", vg)
            op("ac", "activation", gsc2[:, :], gsc1[:, :], AF.Tanh, scale=GELU_C1)
            va2 = bump("ac")
            wait("ve", "sA", va2)
            wait("ve", "sV", v_f1)
            op("ve", "tensor_scalar", gsc2[:, :], gsc2[:, :], 0.5, 0.5, ALU.mult, ALU.add)
            op("ve", "tensor_mul", gsc2[:, :], gsc2[:, :], xgb[:, :])
            op("ve", "tensor_tensor", act2[:, jb * TT1:(jb + 1) * TT1], gsc2[:, :], f1b[:, :], ALU.mult)
            vv = bump("ve")
            used_at("gsc1", "sV", vv)
            used_at("xgb", "sV", vv)
            used_at("f1b", "sV", vv)
            load_wg(jb + 2)
        v_act2 = cV.v
        if TAPS and u == 0:
            wait("sy", "sV", v_act2)
            dma("sy", taps["t_p6b"], act2[:, 0:1024])

        psclose(f"p_ff_u{u-1}" if u > 0 else "p_ff", f"p_fg_u{u-1}" if u > 0 else "p_fg")
        p_f2 = [psopen(f"p_f2_{u}_{i}", [128, 512]) for i in range(8)]
        wait("sy", "sG", wcast["w2"])
        wait("te", "sV", v_act2)
        NJG = 16
        for f2sl in range(4):
            w2_ld = []

            def load_w2(jg):
                if jg < NJG:
                    guard("sy", f"w2b{jg % 2}")
                    w2_ld.append(dma("sy", w2b[jg % 2][:, :],
                                     w2_bf[jg * 512:(jg + 1) * 512,
                                           f2sl * 512:(f2sl + 1) * 512].rearrange("(j p) c -> p j c", p=128)))
                else:
                    w2_ld.append(0)

            x2sll = []

            def load_x2sl(tb):
                if tb < 8:
                    guard("sy", f"x2sl{tb % 2}")
                    x2sll.append(dma("sy", x2sl[tb % 2][:, :],
                                     sp_x2[u * 8 + tb, :, f2sl * 512:(f2sl + 1) * 512]))
                else:
                    x2sll.append(0)

            for jg in range(2):
                load_w2(jg)
            for tb in range(2):
                load_x2sl(tb)
            for tb in range(8):
                guard("te", f"p_f2_{tb}")
            for jg in range(NJG):
                wait("te", "sS", w2_ld[jg])
                for ji in range(4):
                    jb = jg * 4 + ji
                    for tb in range(8):
                        op("te", "matmul", p_f2[tb][:, :],
                           act2[:, jb * TT1 + tb * 128: jb * TT1 + (tb + 1) * 128],
                           w2b[jg % 2][:, ji * 512:(ji + 1) * 512],
                           start=(jb == 0), stop=(jb == NJB - 1))
                vt_g = bump("te")
                used_at(f"w2b{jg % 2}", "sT", vt_g)
                load_w2(jg + 2)
            vt = cT.v
            for tb in range(8):
                wait("ve", "sT", vt)
                wait("ve", "sS", x2sll[tb])
                guard("ve", "obuf")
                op("ve", "tensor_copy", fob[:, :], p_f2[tb][:, :])
                op("ve", "tensor_add", obuf[:, :], fob[:, :], x2sl[tb % 2][:, :])
                vv = bump("ve")
                used_at(f"p_f2_{tb}", "sV", vv)
                used_at(f"x2sl{tb % 2}", "sV", vv)
                wait("sy", "sV", vv)
                w_o = dma("sy", out[u * TT1 + tb * 128: u * TT1 + (tb + 1) * 128,
                                    f2sl * 512:(f2sl + 1) * 512], obuf[:, :])
                used_at("obuf", "sS", w_o)
                load_x2sl(tb + 2)
        psclose(*[f"p_f2_{u}_{i}" for i in range(7, -1, -1)])
        phase_gate()
        p_fg = psopen(f"p_fg_u{u}", [128, TT1])
        p_ff = psopen(f"p_ff_u{u}", [128, TT1])
        used_at("p_fg", "sA", 0)
        used_at("p_ff", "sV", 0)
        used_at("x2lt", "sT", cT.v)
        used_at("act2", "sT", cT.v)
    psclose("p_ff_u1", "p_fg_u1")

    return finish(nc, Q, sem, cS, ctr, cc_in, cc_out, ctx)


def finish(nc, Q, sem, cS, ctr, cc_in, cc_out, ctx):
    Q["sy"].append(("wait", "sS", cS.v))

    def run_queue(e, q):
        for item in Q[q]:
            kind = item[0]
            if kind == "op":
                inst = getattr(e, item[1])(*item[2], **item[3])
                for sname in item[4]:
                    inst = inst.then_inc(sem[sname], 1)
            elif kind == "wait":
                e.wait_ge(sem[item[1]], item[2])
            elif kind == "dma":
                inst = e.dma_start(out=item[1], in_=item[2]).then_inc(sem[item[3]], 16)
                for sname in item[4]:
                    inst = inst.then_inc(sem[sname], 1)
            elif kind == "cc":
                inst = e.collective_compute(
                    "AllGather", mybir.AluOpType.bypass, replica_groups=[[0, 1, 2, 3], [4, 5, 6, 7]],
                    ins=[cc_in.opt()], outs=[cc_out.opt()]).then_inc(sem["sC"], 1)
                for sname in item[-1]:
                    inst = inst.then_inc(sem[sname], 1)

    ALU_ = mybir.AluOpType
    with nc.Block() as block:
        @block.sync
        def _(e):
            run_queue(e, "sy")

        @block.gpsimd
        def _(e):
            run_queue(e, "gp")

        @block.vector
        def _(e):
            run_queue(e, "ve")

        @block.tensor
        def _(e):
            run_queue(e, "te")

        @block.scalar
        def _(e):
            run_queue(e, "ac")

    ctx.close()
    return nc


def _host_consts():
    bf = ml_dtypes.bfloat16
    identf = np.eye(128, dtype=np.float32)
    identb = np.eye(128, dtype=bf)
    triu = np.triu(np.ones((128, 128), np.float32))
    ones128b = np.ones((128, 1), dtype=bf)
    diffL = np.zeros((97, 2048), np.float32)
    diffL[64, :] = 1.0
    s_idx = np.arange(128)
    masklhs = np.zeros((32, 128), np.float32)
    for b in range(32):
        masklhs[b, s_idx // 4 == b] = 30000.0
    diffL[65:97, :] = np.tile(masklhs, (1, 16))
    diffR = np.zeros((97, NH * L), np.float32)
    for hp in range(64):
        diffR[hp, hp * L:(hp + 1) * L] = -1.0
    l_idx = np.arange(L)
    for b in range(32):
        row = np.zeros((L,), np.float32)
        row[l_idx // 4 < b] = -1.0
        diffR[65 + b, :] = np.tile(row, NH)
    return identf, identb, triu, ones128b, diffL, diffR


def kernel(L0_attn_hidden_states, L0_attn_W_in, L0_attn_conv_w, L0_attn_conv_b,
           L0_attn_dt_bias, L0_attn_A, L0_attn_D, L0_attn_norm_weight,
           L0_attn_W_out, L0_ffn_w_gating, L0_ffn_w_linear):
    bf = ml_dtypes.bfloat16
    x = np.ascontiguousarray(np.asarray(L0_attn_hidden_states, dtype=np.float32))
    W_in = np.ascontiguousarray(np.asarray(L0_attn_W_in, dtype=np.float32))
    conv_w = np.asarray(L0_attn_conv_w, dtype=np.float32)
    conv_b = np.asarray(L0_attn_conv_b, dtype=np.float32)
    dt_bias = np.asarray(L0_attn_dt_bias, dtype=np.float32)
    A = np.asarray(L0_attn_A, dtype=np.float32)
    D = np.asarray(L0_attn_D, dtype=np.float32)
    norm_w = np.asarray(L0_attn_norm_weight, dtype=np.float32)
    W_out = np.ascontiguousarray(np.asarray(L0_attn_W_out, dtype=np.float32))
    w_g = np.asarray(L0_ffn_w_gating, dtype=np.float32)
    w_l = np.ascontiguousarray(np.asarray(L0_ffn_w_linear, dtype=np.float32))

    identf, identb, triu, ones128b, diffL, diffR = _host_consts()
    wconv = np.ascontiguousarray(conv_w.reshape(NCHB_BC, 128, KC).transpose(1, 0, 2).reshape(128, NCHB_BC * KC))
    convb = np.ascontiguousarray(conv_b.reshape(NCHB_BC, 128).T)
    normw = np.ascontiguousarray(norm_w.reshape(NCHB_X, 128).T)
    Dbc = np.ascontiguousarray(np.tile(D[None, :], (128, 1)).astype(bf))
    Wg0 = np.ascontiguousarray(w_g[0])
    Wg1 = np.ascontiguousarray(w_g[1])

    in_maps = []
    for k in range(8):
        b, q = k // 4, k % 4
        xhv = np.zeros((TH, HID), np.float32)
        t0 = q * T
        if q > 0:
            xhv[0:3] = x[b, t0 - 3:t0]
        xhv[3:] = x[b, t0:t0 + T]
        cm = np.zeros((8,), np.float32)
        for j in range(3):
            cm[j] = 1.0 if j < q else 0.0
        for kk in (1, 2):
            cm[3 + kk - 1] = 1.0 if kk < q else 0.0
        cmask = np.ascontiguousarray(np.tile(cm[None, :], (64, 1)))
        in_maps.append({
            "xh": np.ascontiguousarray(xhv), "W_in": W_in, "Wg0": Wg0, "Wg1": Wg1,
            "W2d": w_l, "Woutd": W_out, "wconv": wconv, "convb": convb,
            "dtbias": np.ascontiguousarray(dt_bias[:, None]),
            "Acol": np.ascontiguousarray(A[:, None]), "Dbc_d": Dbc,
            "normw": normw, "cmask": cmask,
            "identf_d": identf, "identb_d": identb, "triu_d": triu,
            "ones128b_d": ones128b, "diffL_d": diffL, "diffR_d": diffR,
        })

    nc = build_nc()
    res = run_bass_kernel_spmd(nc, in_maps, core_ids=list(range(8)))
    outp = np.zeros((2, 8192, HID), np.float32)
    for k in range(8):
        b, q = k // 4, k % 4
        outp[b, q * T:(q + 1) * T] = res.results[k]["out"]
    if TAPS:
        kernel.taps = [res.results[k] for k in range(8)]
    return outp


# version 07
# speedup vs baseline: 1.0392x; 1.0392x over previous
import os
import contextlib
import numpy as np
import ml_dtypes
import concourse.bass as bass
import concourse.mybir as mybir
from concourse.bass_utils import run_bass_kernel_spmd

F32 = mybir.dt.float32
BF16 = mybir.dt.bfloat16
FP8 = mybir.dt.float8e4
AF = mybir.ActivationFunctionType
ALU = mybir.AluOpType

HID = 2048
DIN = 8512
INTER = 4096
GN = 128
NH = 64
HD = 64
KC = 4
CONVD = INTER + 2 * GN
T = 2048
TH = T + 3
L = 128
NCH = T // L
NFB = HID // 128
NCHB_X = INTER // 128
NCHB_BC = CONVD // 128
NGATE = INTER // 128
FFH = 8192
NJB = FFH // 128
TT1 = 1024
EPS = 1e-5
GELU_C1 = float(np.sqrt(2.0 / np.pi))
GELU_C2 = 0.044715

TAPS = os.environ.get("KTAPS", "") == "1"
KSTOP = os.environ.get("KSTOP", "")
KLN_ID = os.environ.get("KLN_ID", "") == "1"
KCHB1 = os.environ.get("KCHB1", "") == "1"


def build_nc():
    nc = bass.Bass("TRN2", target_bir_lowering=False, debug=False)

    dai = lambda n, s, d=F32: nc.dram_tensor(n, s, d, kind="ExternalInput").ap()
    dao = lambda n, s, d=F32: nc.dram_tensor(n, s, d, kind="ExternalOutput").ap()
    dint = lambda n, s, d=F32: nc.dram_tensor(n, s, d, kind="Internal").ap()

    xh = dai("xh", [TH, HID])
    W_in = dai("W_in", [HID, DIN])
    Wg0 = dai("Wg0", [HID, FFH]); Wg1 = dai("Wg1", [HID, FFH])
    W2d = dai("W2d", [FFH, HID])
    Woutd = dai("Woutd", [INTER, HID])
    wconv = dai("wconv", [128, NCHB_BC * KC])
    convb = dai("convb", [128, NCHB_BC])
    dtbias = dai("dtbias", [NH, 1])
    Acol = dai("Acol", [NH, 1])
    Dbc_d = dai("Dbc_d", [128, NH], BF16)
    normw = dai("normw", [128, NCHB_X])
    cmask = dai("cmask", [64, 8])
    identf_d = dai("identf_d", [128, 128])
    identb_d = dai("identb_d", [128, 128], BF16)
    triu_d = dai("triu_d", [128, 128])
    ones128b_d = dai("ones128b_d", [128, 1], BF16)
    diffL_d = dai("diffL_d", [97, 2048])
    diffR_d = dai("diffR_d", [97, NH * L])

    out = dao("out", [T, HID])
    taps = {}
    if TAPS:
        taps["t_xlnT"] = dao("t_xlnT", [128, NFB * TH], BF16)
        taps["t_dt"] = dao("t_dt", [NH, TH])
        taps["t_B"] = dao("t_B", [128, T], BF16)
        taps["t_C"] = dao("t_C", [128, T], BF16)
        taps["t_y"] = dao("t_y", [128, INTER], BF16)
        taps["t_z"] = dao("t_z", [128, NCHB_X * 128], BF16)
        taps["t_x2"] = dao("t_x2", [128, HID])
        taps["t_glob"] = dao("t_glob", [128, INTER])
        taps["t_xbc"] = dao("t_xbc", [128, INTER], BF16)
        taps["t_st"] = dao("t_st", [128, 4 * 17])
        taps["t_carry"] = dao("t_carry", [128, INTER])
        taps["t_xw"] = dao("t_xw", [128, INTER], BF16)
        taps["t_dbc"] = dao("t_dbc", [128, NH])
        taps["t_cin"] = dao("t_cin", [129, INTER])
        taps["t_misc"] = dao("t_misc", [128, 80])
        taps["t_p6a"] = dao("t_p6a", [128, 1024], FP8)
        taps["t_p6b"] = dao("t_p6b", [128, 1024], FP8)

    sp_xT = dint("sp_xT", [NCH, 128, INTER], BF16)
    sp_carry = dint("sp_carry", [NCH, 128, INTER])
    sp_sg = dint("sp_sg", [NGATE, 128, T], BF16)
    sp_z = dint("sp_z", [NCHB_X, 128, T], BF16)
    sp_x2 = dint("sp_x2", [NCH, 128, HID])
    sp_x2lnT = dint("sp_x2lnT", [NFB, 128, T], FP8)
    wg0_bf = dint("wg0_bf", [HID, FFH], FP8)
    wg1_bf = dint("wg1_bf", [HID, FFH], FP8)
    w2_bf = dint("w2_bf", [FFH, HID], FP8)
    wout_bf = dint("wout_bf", [INTER, HID], BF16)
    cc_in = dint("cc_in", [129, INTER])
    cc_out = dint("cc_out", [4 * 129, INTER])

    Q = {k: [] for k in ("te", "ve", "ac", "gp", "sy")}

    class Cnt:
        def __init__(self):
            self.v = 0

    cS, cG, cT, cV, cA, cP, cC, cW = (Cnt() for _ in range(8))
    ctr = {"sS": cS, "sG": cG, "sT": cT, "sV": cV, "sA": cA, "sP": cP, "sC": cC,
           "sW": cW}
    qsem = {"te": ("sT", cT), "ve": ("sV", cV), "ac": ("sA", cA), "gp": ("sP", cP)}

    ctx = contextlib.ExitStack()
    sem = {name: ctx.enter_context(nc.semaphore(name)) for name in ctr}

    dirty_w = {"ve": set(), "gp": set()}
    dirty_r = {"ve": set(), "gp": set()}

    def _apname(a):
        try:
            return a.tensor.name
        except AttributeError:
            return None

    def op(q, meth, *args, **kw):
        if q in dirty_w:
            outn = _apname(args[0]) if args else None
            rds = {n for n in (_apname(a) for a in args[1:]) if n}
            for kwv in kw.values():
                n = _apname(kwv)
                if n:
                    rds.add(n)
            hazard = (rds & dirty_w[q]) or (outn in dirty_w[q]) or (outn in dirty_r[q])
            if hazard:
                sname, c = qsem[q]
                for item in reversed(Q[q]):
                    if item[0] in ("op", "cc"):
                        if sname not in item[-1]:
                            c.v += 1
                            item[-1].append(sname)
                        break
                Q[q].append(["wait", sname, c.v])
                dirty_w[q].clear()
                dirty_r[q].clear()
            if outn:
                dirty_w[q].add(outn)
            dirty_r[q] |= rds
        Q[q].append(["op", meth, args, kw, []])

    def wait(q, s, v):
        if v > 0:
            Q[q].append(["wait", s, v])

    def bump(q):
        s, c = qsem[q]
        for item in reversed(Q[q]):
            if item[0] in ("op", "cc"):
                if s not in item[-1]:
                    c.v += 1
                    item[-1].append(s)
                return c.v
        raise AssertionError(f"bump({q}) with no prior instruction")

    def dma(q, dst, src, semname=None):
        s = semname or ("sS" if q == "sy" else "sG")
        c = ctr[s]
        c.v += 16
        Q[q].append(["dma", dst, src, s, []])
        return c.v

    guards = {}

    def used_at(buf, s, v):
        guards[buf] = [(s, v)]

    def used_also(buf, s, v):
        guards.setdefault(buf, []).append((s, v))

    def guard(q, buf):
        for g in guards.get(buf, ()):
            wait(q, g[0], g[1])

    sbuf_stack = []

    def sb(n, shape, d=F32):
        st = contextlib.ExitStack()
        t = st.enter_context(nc.sbuf_tensor(n, shape, d))
        sbuf_stack.append((n, st))
        return t

    def sbclose(*names):
        for nme in names:
            n2, st = sbuf_stack[-1]
            assert n2 == nme, f"sbuf close order: want {nme}, top is {n2}"
            st.close()
            sbuf_stack.pop()

    psum_stack = []

    def psopen(n, shape, d=F32):
        st = contextlib.ExitStack()
        t = st.enter_context(nc.psum_tensor(n, shape, d))
        psum_stack.append((n, st))
        return t

    def psclose(*names):
        for nme in names:
            n2, st = psum_stack[-1]
            assert n2 == nme, f"psum close order: want {nme}, top is {n2}"
            st.close()
            psum_stack.pop()

    def phase_gate():
        for q in ("sy", "gp", "ve", "te", "ac"):
            for sn, c in (("sT", cT), ("sV", cV), ("sA", cA), ("sP", cP),
                          ("sS", cS), ("sG", cG)):
                wait(q, sn, c.v)
        for q in ("ve", "gp"):
            dirty_w[q].clear()
            dirty_r[q].clear()

    identf = sb("identf", [128, 128])
    identb = sb("identb", [128, 128], BF16)
    triu_f = sb("triu_f", [128, 128])
    ones128b = sb("ones128b", [128, 1], BF16)
    ones64f = sb("ones64f", [64, 128])
    wconv_s = sb("wconv_s", [128, NCHB_BC * KC])
    convb_s = sb("convb_s", [128, NCHB_BC])
    convbh_s = sb("convbh_s", [128, NCHB_BC])
    dtbias_s = sb("dtbias_s", [NH, 1])
    Acol_s = sb("Acol_s", [NH, 1])
    Dbc_s = sb("Dbc_s", [128, NH], BF16)
    normw_s = sb("normw_s", [128, NCHB_X])
    cmask_s = sb("cmask_s", [64, 8])
    sq_scr = sb("sq_scr", [128, HID], BF16)
    dbg = sb("dbg", [128, 80]) if TAPS else None
    NTB = 17
    stat_m = sb("stat_m", [128, NTB]); stat_q = sb("stat_q", [128, NTB])
    rstd1 = sb("rstd1", [128, NTB]); bias1 = sb("bias1", [128, NTB])
    hscr1 = sb("hscr1", [128, NTB]); hscr2 = sb("hscr2", [128, NTB])
    dcols = sb("dcols", [NH, NCH + 1])
    Dprod = sb("Dprod", [NH, NCH + 1])
    varz = sb("varz", [128, NCH])
    rstdz = sb("rstdz", [128, NCH])
    stat2_m = sb("stat2_m", [128, NCH]); stat2_q = sb("stat2_q", [128, NCH])
    rstd2 = sb("rstd2", [128, NCH]); bias2 = sb("bias2", [128, NCH])
    dAbuf = sb("dAbuf", [NH, L])
    diagtmp = sb("diagtmp", [64, 64])
    vrow = sb("vrow", [1, 128])
    prow = sb("prow", [1, 64])
    pcol = sb("pcol", [NH, 4]); coefc = sb("coefc", [NH, 4]); cbc = sb("cbc", [128, NH])
    decayf = sb("decayf", [NH, L]); wstf = sb("wstf", [NH, L])
    wstTM = sb("wstTM", [128, NH], BF16)
    BtTM = sb("BtTM", [128, 128], BF16)
    dbc = sb("dbc", [128, NH])

    for dst, src in ((identf, identf_d), (identb, identb_d), (triu_f, triu_d),
                     (ones128b, ones128b_d), (wconv_s, wconv), (convb_s, convb),
                     (dtbias_s, dtbias), (Acol_s, Acol), (Dbc_s, Dbc_d),
                     (normw_s, normw), (cmask_s, cmask)):
        dma("sy", dst[:, :], src)
    v_const = cS.v
    wait("ve", "sS", v_const)
    op("ve", "memset", ones64f[:, :], 1.0)
    op("ve", "tensor_scalar_mul", convbh_s[:, :], convb_s[:, :], 0.5)
    op("ve", "memset", stat_q[:, :], 0.0)
    op("ve", "memset", stat2_q[:, :], 0.0)
    op("ve", "memset", stat_m[:, :], 0.0)
    op("ve", "memset", stat2_m[:, :], 0.0)
    v_const_ve = bump("ve")

    wcast = {}

    def emit_heron(q, out_ap, var_ap, s1, s2, niter=5):
        BIG = 1.0e6
        op(q, "memset", out_ap, 1.0)
        for k in range(-17, 21):
            sc = float(2.0 ** (-k))
            op(q, "tensor_scalar", s1, var_ap, BIG * sc, -BIG, ALU.mult, ALU.add)
            op(q, "tensor_scalar_min", s2, s1, 1.0)
            op(q, "tensor_scalar_max", s1, s2, 0.0)
            op(q, "tensor_scalar", s2, s1, -(1.0 - 2.0 ** -0.5), 1.0, ALU.mult, ALU.add)
            op(q, "tensor_mul", out_ap, out_ap, s2)
        op(q, "tensor_scalar_mul", out_ap, out_ap, float(2.0 ** 8.5))
        for _ in range(niter):
            op(q, "tensor_mul", s1, out_ap, out_ap)
            op(q, "tensor_mul", s1, s1, var_ap)
            op(q, "tensor_scalar", s1, s1, -0.5, 1.5, ALU.mult, ALU.add)
            op(q, "tensor_mul", out_ap, out_ap, s1)

    B_FM = sb("B_FM", [128, T], BF16)
    C_FM = sb("C_FM", [128, T], BF16)
    dtv = sb("dtv", [NH, TH])
    diffL = sb("diffL", [97, 2048])
    dtTM = sb("dtTM", [128, NCH * NH], BF16)
    uall = sb("uall", [128, NCH * NH], BF16)
    xTb = [sb(f"xTb{i}", [128, INTER], BF16) for i in range(2)]
    globalC = sb("globalC", [128, INTER])

    xlnT = sb("xlnT", [128, NFB * TH], BF16)
    xtile = [sb(f"xtile{i}", [128, HID]) for i in range(2)]
    xln_tm = sb("xln_tm", [128, HID], BF16)

    p_trb = psopen("p_trb", [128, 128], BF16)
    p_trb2 = psopen("p_trb2", [128, 128], BF16)
    p_ms = psopen("p_ms", [128, 128])
    p_trf = psopen("p_trf", [128, 128])

    tb_ranges = [(128 * i, 128) for i in range(16)] + [(TH - 128, 128)]

    for i, (t0, tn) in enumerate(tb_ranges):
        buf = xtile[i % 2]
        guard("sy", f"xtile{i % 2}")
        w = dma("sy", buf[:, :], xh[t0:t0 + tn, :])
        wait("ve", "sS", w)
        op("ve", "reduce_sum", stat_m[:, i:i + 1], buf[:, :], mybir.AxisListType.X)
        vv_r = bump("ve")
        wait("ac", "sS", w)
        wait("ac", "sV", v_const_ve)
        op("ac", "activation", sq_scr[:, :], buf[:, :], AF.Square, accum_out=stat_q[:, i:i + 1])
        va = bump("ac")
        used_at(f"xtile{i % 2}", "sA", va)
        used_also(f"xtile{i % 2}", "sV", vv_r)
    v_stats_a = cA.v
    v_stats_v = cV.v

    wait("ve", "sA", v_stats_a)
    wait("ve", "sV", v_stats_v)
    op("ve", "tensor_scalar_mul", stat_m[:, :], stat_m[:, :], 1.0 / HID)
    op("ve", "tensor_scalar_mul", stat_q[:, :], stat_q[:, :], 1.0 / HID)
    op("ve", "tensor_mul", hscr1[:, :], stat_m[:, :], stat_m[:, :])
    op("ve", "tensor_sub", stat_q[:, :], stat_q[:, :], hscr1[:, :])
    op("ve", "tensor_scalar_add", stat_q[:, :], stat_q[:, :], EPS)
    emit_heron("ve", rstd1[:, :], stat_q[:, :], hscr1[:, :], hscr2[:, :])
    op("ve", "tensor_mul", bias1[:, :], stat_m[:, :], rstd1[:, :])
    op("ve", "tensor_scalar_mul", bias1[:, :], bias1[:, :], -1.0)
    v_rstd1 = bump("ve")

    wait("ac", "sV", v_rstd1)
    for i, (t0, tn) in enumerate(tb_ranges):
        buf = xtile[i % 2]
        guard("sy", f"xtile{i % 2}")
        w = dma("sy", buf[:, :], xh[t0:t0 + tn, :])
        wait("ac", "sS", w)
        guard("ac", "xln_tm")
        if KLN_ID:
            op("ac", "activation", xln_tm[:, :], buf[:, :], AF.Identity, bias=0.0, scale=1.0)
        else:
            op("ac", "activation", xln_tm[:, :], buf[:, :], AF.Identity,
               bias=bias1[:, i:i + 1], scale=rstd1[:, i:i + 1])
        va = bump("ac")
        used_at(f"xtile{i % 2}", "sA", va)
        wait("te", "sA", va)
        wait("te", "sV", v_const_ve)
        for fb in range(NFB):
            pt, pname = (p_trb, "p_trb") if fb % 2 == 0 else (p_trb2, "p_trb2")
            guard("te", pname)
            op("te", "transpose", pt[:, :], xln_tm[:, fb * 128:(fb + 1) * 128], identb[:, :])
            vt = bump("te")
            wait("ve", "sT", vt)
            op("ve", "tensor_copy", xlnT[:, fb * TH + t0: fb * TH + t0 + tn], pt[:, :])
            vv = bump("ve")
            used_at(pname, "sV", vv)
        used_at("xln_tm", "sT", cT.v)
    if TAPS:
        wait("sy", "sV", cV.v)
        dma("sy", taps["t_xlnT"], xlnT[:, :])
        dma("sy", taps["t_st"][:, 0:17], stat_m[:, :])
        dma("sy", taps["t_st"][:, 17:34], stat_q[:, :])
        dma("sy", taps["t_st"][:, 34:51], rstd1[:, :])
        dma("sy", taps["t_st"][:, 51:68], bias1[:, :])
    if KSTOP == "1":
        return finish(nc, Q, sem, cS, ctr, cc_in, cc_out, ctx)
    sbclose("xln_tm", "xtile1", "xtile0")
    phase_gate()

    w_diffL = dma("sy", diffL[:, :], diffL_d)

    NSTRIP = 2
    wbuf = [sb(f"wbuf{i}", [128, NSTRIP * NFB * 128], BF16) for i in range(2)]
    xbc_slab = sb("xbc_slab", [128, TH + 1], BF16)
    vbuf = sb("vbuf", [128, 512], BF16)
    tnbuf = sb("tnbuf", [128, 512], BF16)
    qbuf = sb("qbuf", [128, 512], BF16)
    xchb = sb("xchb", [128, T], BF16)
    diag_t = [sb(f"diag{k}", [128, 128], BF16) for k in range(KC)]
    xstage = sb("xstage", [128, 4 * 128], BF16)
    sg_chb = sb("sg_chb", [128, T], BF16)

    p_pr = [psopen(f"p_pr{i}", [128, 512]) for i in range(2)]

    slices_all = [(0, 3, True), (3, 512, False), (515, 512, False), (1027, 512, False), (1539, 512, False)]

    def wtile(si, fb, j, nchb):
        return wbuf[si % 2][:, (fb * nchb + j) * 128:(fb * nchb + j) * 128 + 128]

    def emit_proj_chb(si, j, nchb, dst_slab, is_dt=False, skip_halo=False, dst_off=0):
        for snum, (c0, cn, halo_only) in enumerate(slices_all):
            if halo_only and skip_halo:
                continue
            par = snum % 2
            pname = f"p_pr{par}"
            guard("te", pname)
            rows = 64 if is_dt else 128
            for fb in range(NFB):
                lh = (wbuf[0][:, fb * 64: fb * 64 + 64] if is_dt
                      else wtile(si, fb, j, nchb))
                op("te", "matmul", p_pr[par][0:rows, 0:cn], lh,
                   xlnT[:, fb * TH + c0: fb * TH + c0 + cn],
                   start=(fb == 0), stop=(fb == NFB - 1))
            vt = bump("te")
            if is_dt:
                wait("ac", "sT", vt)
                op("ac", "activation", dst_slab[:, c0 - dst_off: c0 - dst_off + cn],
                   p_pr[par][0:rows, 0:cn], AF.Identity, bias=dtbias_s[:, 0:1], scale=1.0)
                va = bump("ac")
                used_at(pname, "sA", va)
            else:
                wait("ve", "sT", vt)
                op("ve", "tensor_copy", dst_slab[:, c0 - dst_off: c0 - dst_off + cn],
                   p_pr[par][0:rows, 0:cn])
                vv = bump("ve")
                used_at(pname, "sV", vv)

    def emit_silu(psum_ap, pname, b_col, bh_col, dst_ap, dn):
        guard("ac", "vbuf")
        if b_col is None:
            op("ac", "activation", vbuf[:, 0:dn], psum_ap, AF.Copy)
            op("ac", "activation", tnbuf[:, 0:dn], psum_ap, AF.Tanh, scale=0.5)
        else:
            op("ac", "activation", vbuf[:, 0:dn], psum_ap, AF.Identity, bias=b_col, scale=1.0)
            op("ac", "activation", tnbuf[:, 0:dn], psum_ap, AF.Tanh, bias=bh_col, scale=0.5)
        va = bump("ac")
        used_at(pname, "sA", va)
        wait("ve", "sA", va)
        op("ve", "tensor_scalar", qbuf[:, 0:dn], tnbuf[:, 0:dn], 0.5, 0.5, ALU.mult, ALU.add)
        op("ve", "tensor_mul", dst_ap, qbuf[:, 0:dn], vbuf[:, 0:dn])
        vv = bump("ve")
        used_at("vbuf", "sV", vv)
        return vv

    xbc_strips = [(s, min(NSTRIP, NCHB_BC - s)) for s in range(0, NCHB_BC, NSTRIP)]
    if KCHB1:
        xbc_strips = [(0, 1)]
    strip_w = []

    def load_xbc_strip(si):
        if si < len(xbc_strips):
            chb0, nchb = xbc_strips[si]
            guard("gp", f"wbuf{si % 2}")
            src = W_in[:, 4096 + chb0 * 128: 4096 + (chb0 + nchb) * 128].rearrange("(fb p) c -> p fb c", p=128)
            strip_w.append(dma("gp", wbuf[si % 2][:, 0:NFB * nchb * 128], src))

    for si in range(2):
        load_xbc_strip(si)

    for si, (chb0, nchb) in enumerate(xbc_strips):
        wait("te", "sG", strip_w[min(1, len(xbc_strips) - 1)] if si == 0 else strip_w[si])
        for j in range(nchb):
            chb = chb0 + j
            guard("ve", "xbc_slab")
            emit_proj_chb(si, j, nchb, xbc_slab)
            wait("ve", "sS", v_const)
            for k in range(KC):
                guard("ve", f"diag{k}")
                op("ve", "tensor_tensor", diag_t[k][:, :], identf[:, :],
                   wconv_s[:, chb * KC + k: chb * KC + k + 1].broadcast_to([128, 128]), ALU.mult)
            v_diag = bump("ve")
            wait("te", "sV", v_diag)
            is_b = chb == 32
            is_c = chb == 33
            dst = B_FM if is_b else C_FM if is_c else xchb
            if not (is_b or is_c):
                guard("ve", "xchb")
            for csl in range(4):
                par = csl % 2
                pname = f"p_pr{par}"
                guard("te", pname)
                for k in range(KC):
                    op("te", "matmul", p_pr[par][:, :], diag_t[k][:, :],
                       xbc_slab[:, csl * 512 + k: csl * 512 + k + 512],
                       start=(k == 0), stop=(k == KC - 1))
                vt = bump("te")
                wait("ac", "sT", vt)
                emit_silu(p_pr[par][:, :], pname, convb_s[:, chb:chb + 1], convbh_s[:, chb:chb + 1],
                          dst[:, csl * 512: csl * 512 + 512], 512)
            used_at("xbc_slab", "sT", cT.v)
            for k in range(KC):
                used_at(f"diag{k}", "sT", cT.v)

            if not (is_b or is_c):
                wait("te", "sV", cV.v)
                for cg in range(4):
                    for cc in range(4):
                        c = cg * 4 + cc
                        pt, pname = (p_trb, "p_trb") if cc % 2 == 0 else (p_trb2, "p_trb2")
                        guard("te", pname)
                        op("te", "transpose", pt[:, :], xchb[:, c * 128:(c + 1) * 128], identb[:, :])
                        vt = bump("te")
                        wait("ve", "sT", vt)
                        guard("ve", "xstage")
                        op("ve", "tensor_copy", xstage[:, cc * 128:(cc + 1) * 128], pt[:, :])
                        vv = bump("ve")
                        used_at(pname, "sV", vv)
                    wait("sy", "sV", cV.v)
                    w = dma("sy", sp_xT[cg * 4:(cg + 1) * 4, :, chb * 128:(chb + 1) * 128].transpose([1, 0, 2]),
                            xstage[:, :])
                    used_at("xstage", "sS", w)
                used_at("xchb", "sT", cT.v)
        used_at(f"wbuf{si % 2}", "sT", cT.v)
        load_xbc_strip(si + 2)

    if TAPS:
        wait("sy", "sV", cV.v)
        dma("sy", taps["t_B"], B_FM[:, :])
        dma("sy", taps["t_C"], C_FM[:, :])

    if KSTOP == "2a":
        return finish(nc, Q, sem, cS, ctr, cc_in, cc_out, ctx)
    if TAPS:
        wait("sy", "sV", cV.v)
        dma("sy", taps["t_misc"][:, 32:80], dbg[:, 32:80])
    guard("gp", "wbuf0")
    wdt = dma("gp", wbuf[0][:, 0:NFB * 64],
              W_in[:, 8448:8512].rearrange("(fb p) c -> p fb c", p=128))
    wait("ve", "sS", v_const)
    op("ve", "memset", dtv[:, :], 0.0)
    v_dtv0 = bump("ve")
    wait("te", "sG", wdt)
    wait("ac", "sV", v_dtv0)
    emit_proj_chb(0, 0, 1, dtv, is_dt=True, skip_halo=True)
    used_at("wbuf0", "sT", cT.v)

    spw = sb("spw", [NH, T]); spy = sb("spy", [NH, T]); spe = sb("spe", [NH, T])
    op("ac", "activation", spe[:, :], dtv[:, 3:TH], AF.Abs)
    op("ac", "activation", spe[:, :], spe[:, :], AF.Exp, scale=-1.0)
    va = bump("ac")
    wait("ve", "sA", va)
    op("ve", "tensor_scalar_add", spw[:, :], spe[:, :], 1.0)
    op("ve", "tensor_scalar_mul", spy[:, :], spe[:, :], 0.7)
    vv = bump("ve")
    for _ in range(4):
        wait("ac", "sV", vv)
        op("ac", "activation", spe[:, :], spy[:, :], AF.Exp, scale=-1.0)
        va = bump("ac")
        wait("ve", "sA", va)
        op("ve", "tensor_mul", spe[:, :], spe[:, :], spw[:, :])
        op("ve", "tensor_scalar_add", spy[:, :], spy[:, :], -1.0)
        op("ve", "tensor_add", spy[:, :], spy[:, :], spe[:, :])
        vv = bump("ve")
    wait("ac", "sV", vv)
    op("ac", "activation", spe[:, :], dtv[:, 3:TH], AF.Relu)
    va = bump("ac")
    wait("ve", "sA", va)
    op("ve", "tensor_add", dtv[:, 3:TH], spy[:, :], spe[:, :])
    v_dt = bump("ve")
    if TAPS:
        wait("sy", "sV", v_dt)
        dma("sy", taps["t_dt"], dtv[:, :])
    sbclose("spe", "spy", "spw")
    phase_gate()

    wait("ve", "sS", w_diffL)
    wait("ve", "sV", v_dt)
    for c in range(NCH):
        op("ve", "tensor_scalar", dAbuf[:, :], dtv[:, 3 + c * L: 3 + (c + 1) * L],
           Acol_s[:, 0:1], None, ALU.mult)
        op("ve", "tensor_tensor_scan", diffL[0:64, c * L:(c + 1) * L],
           ones64f[:, :], dAbuf[:, :], 0.0, ALU.mult, ALU.add)
    v_cA = bump("ve")
    wait("ac", "sV", v_cA)
    for c in range(NCH):
        op("ac", "activation", dcols[:, c:c + 1], diffL[0:64, c * L + L - 1: c * L + L], AF.Exp)
    v_d = bump("ac")
    wait("ve", "sA", v_d)
    op("ve", "memset", Dprod[:, 0:1], 1.0)
    for c in range(NCH):
        op("ve", "tensor_mul", Dprod[:, c + 1:c + 2], Dprod[:, c:c + 1], dcols[:, c:c + 1])
    v_dp = bump("ve")

    if KSTOP == "2adt":
        return finish(nc, Q, sem, cS, ctr, cc_in, cc_out, ctx)
    carry = sb("carry", [128, INTER])
    xw = sb("xw", [128, INTER], BF16)

    psclose("p_pr1", "p_pr0")
    phase_gate()
    p_st = [psopen(f"p_st{i}", [128, 512]) for i in range(2)]
    p_gate = [psopen(f"p_gate{i}", [128, 512]) for i in range(2)]

    op("ve", "memset", carry[:, :], 0.0)
    bump("ve")
    wait("te", "sA", cA.v)
    wait("te", "sV", cV.v)

    xT_ld = []

    def load_xT(c, tag):
        if c < NCH:
            guard("sy", f"xTb{c % 2}@{tag}")
            xT_ld.append(dma("sy", xTb[c % 2][:, :], sp_xT[c, :, :]))
        else:
            xT_ld.append(0)

    for c in range(2):
        load_xT(c, "p2b")

    for c in range(NCH):
        wait("ac", "sV", v_cA)
        guard("ac", "decayf")
        op("ac", "activation", decayf[:, :], diffL[0:64, c * L:(c + 1) * L], AF.Exp,
           bias=diffL[0:64, c * L + L - 1: c * L + L], scale=-1.0)
        va = bump("ac")
        wait("ve", "sA", va)
        guard("ve", "wstf")
        op("ve", "tensor_mul", wstf[:, :], decayf[:, :], dtv[:, 3 + c * L: 3 + (c + 1) * L])
        vv = bump("ve")
        used_at("decayf", "sV", vv)
        wait("te", "sV", vv)
        guard("te", "p_trf")
        op("te", "transpose", p_trf[:, 0:64], wstf[:, :], identf[0:64, 0:64])
        vt = bump("te")
        used_at("wstf", "sT", vt)
        wait("ve", "sT", vt)
        guard("ve", "wstTM")
        op("ve", "tensor_copy", wstTM[:, :], p_trf[:, 0:64])
        vv1 = bump("ve")
        used_at("p_trf", "sV", vv1)
        guard("te", "p_trf")
        op("te", "transpose", p_trf[:, 0:64], dtv[:, 3 + c * L: 3 + (c + 1) * L], identf[0:64, 0:64])
        vt = bump("te")
        wait("ve", "sT", vt)
        op("ve", "tensor_copy", dtTM[:, c * NH:(c + 1) * NH], p_trf[:, 0:64])
        vv2 = bump("ve")
        used_at("p_trf", "sV", vv2)
        guard("te", "p_trf")
        op("te", "transpose", p_trf[:, 0:64], diffL[0:64, c * L:(c + 1) * L], identf[0:64, 0:64])
        vt = bump("te")
        wait("ac", "sT", vt)
        op("ac", "activation", uall[:, c * NH:(c + 1) * NH], p_trf[:, 0:64], AF.Exp)
        va = bump("ac")
        used_at("p_trf", "sA", va)
        guard("te", "p_trb")
        op("te", "transpose", p_trb[:, :], B_FM[:, c * L:(c + 1) * L], identb[:, :])
        vt = bump("te")
        wait("ve", "sT", vt)
        guard("ve", "BtTM")
        op("ve", "tensor_copy", BtTM[:, :], p_trb[:, :])
        v_bt = bump("ve")
        used_at("p_trb", "sV", v_bt)

        guard("ve", "diagtmp")
        op("ve", "tensor_tensor", diagtmp[:, :], identf[0:64, 0:64],
           diffL[0:64, c * L + L - 1: c * L + L].broadcast_to([64, 64]), ALU.mult)
        v_dg = bump("ve")
        wait("te", "sV", v_dg)
        guard("te", "p_ms")
        op("te", "matmul", p_ms[:, 0:64], ones64f[:, :], diagtmp[:, :], start=True, stop=True)
        vt = bump("te")
        used_at("diagtmp", "sT", vt)
        wait("ac", "sT", vt)
        guard("ac", "dbc")
        op("ac", "activation", dbc[:, :], p_ms[:, 0:64], AF.Exp)
        v_dbc = bump("ac")
        used_at("p_ms", "sA", v_dbc)

        wait("gp", "sS", xT_ld[c])
        wait("gp", "sV", vv1)
        guard("gp", "xw")
        op("gp", "tensor_tensor", xw[:, :], xTb[c % 2][:, :],
           wstTM[:, :].unsqueeze(2).broadcast_to([128, NH, HD]), ALU.mult)
        v_xw = bump("gp")
        used_at("wstTM", "sP", v_xw)
        used_at(f"xTb{c % 2}@p2b", "sP", v_xw)

        wait("sy", "sV", cV.v)
        w_cs = dma("sy", sp_carry[c, :, :], carry[:, :])

        wait("gp", "sA", v_dbc)
        wait("gp", "sS", w_cs)
        wait("gp", "sV", cV.v)
        op("gp", "tensor_tensor", carry[:, :], carry[:, :],
           dbc[:, 0:NH].unsqueeze(2).broadcast_to([128, NH, HD]), ALU.mult)
        v_ct = bump("gp")
        used_at("dbc", "sP", v_ct)
        wait("te", "sV", v_bt)
        wait("te", "sP", v_xw)
        for sl in range(8):
            par = sl % 2
            pname = f"p_st{par}"
            guard("te", pname)
            op("te", "matmul", p_st[par][:, :], BtTM[:, :], xw[:, sl * 512:(sl + 1) * 512],
               start=True, stop=True)
            vt = bump("te")
            wait("ve", "sT", vt)
            wait("ve", "sP", v_ct)
            op("ve", "tensor_add", carry[:, sl * 512:(sl + 1) * 512],
               carry[:, sl * 512:(sl + 1) * 512], p_st[par][:, :])
            vv = bump("ve")
            used_at(pname, "sV", vv)
        used_at("BtTM", "sT", cT.v)
        used_at("xw", "sT", cT.v)
        load_xT(c + 2, "p2b")

    v_cfin = cV.v
    wait("sy", "sV", v_cfin)
    w_cc1 = dma("sy", cc_in[0:128, :], carry[:, :])
    if TAPS:
        dma("sy", taps["t_carry"], carry[:, :])
        dma("sy", taps["t_xw"], xw[:, :])
        dma("sy", taps["t_dbc"], dbc[:, :])
        dma("sy", taps["t_xbc"], sp_xT[15, :, :])
    wait("te", "sV", v_dp)
    guard("te", "p_trf")
    op("te", "transpose", p_trf[0:1, 0:64], Dprod[:, NCH:NCH + 1], identf[0:64, 0:64])
    vt = bump("te")
    wait("ve", "sT", vt)
    op("ve", "tensor_copy", prow[0:1, 0:64], p_trf[0:1, 0:64])
    v_pc = bump("ve")
    used_at("p_trf", "sV", v_pc)
    wait("sy", "sV", v_pc)
    w_cc2 = dma("sy", cc_in[128:129, 0:64], prow[0:1, 0:64])
    wait("gp", "sS", max(w_cc1, w_cc2))
    Q["gp"].append(["cc", []])
    cC.v += 0
    if TAPS:
        wait("sy", "sC", cC.v + 1)
        dma("sy", taps["t_cin"], cc_in[:, :])
    cC.v += 1
    v_cc = cC.v
    sbclose("xw", "carry")
    phase_gate()

    if KSTOP == "2b":
        return finish(nc, Q, sem, cS, ctr, cc_in, cc_out, ctx)
    gate_strips = [(s, NSTRIP) for s in range(0, NGATE, NSTRIP)]
    gstrip_w = []

    def load_gate_strip(si):
        if si < len(gate_strips):
            chb0, nchb = gate_strips[si]
            guard("gp", f"wbuf{si % 2}")
            src = W_in[:, chb0 * 128: (chb0 + nchb) * 128].rearrange("(fb p) c -> p fb c", p=128)
            gstrip_w.append(dma("gp", wbuf[si % 2][:, 0:NFB * nchb * 128], src))

    for si in range(2):
        load_gate_strip(si)
    for si, (chb0, nchb) in enumerate(gate_strips):
        wait("te", "sG", gstrip_w[min(1, len(gate_strips) - 1)] if si == 0 else gstrip_w[si])
        for j in range(nchb):
            chb = chb0 + j
            guard("ve", "sg_chb")
            for snum, (c0, cn, halo_only) in enumerate(slices_all):
                if halo_only:
                    continue
                par = snum % 2
                pname = f"p_gate{par}"
                guard("te", pname)
                for fb in range(NFB):
                    op("te", "matmul", p_gate[par][:, 0:cn], wtile(si, fb, j, nchb),
                       xlnT[:, fb * TH + c0: fb * TH + c0 + cn],
                       start=(fb == 0), stop=(fb == NFB - 1))
                vt = bump("te")
                wait("ac", "sT", vt)
                emit_silu(p_gate[par][:, 0:cn], pname, None, None,
                          sg_chb[:, c0 - 3: c0 - 3 + cn], cn)
            wait("sy", "sV", cV.v)
            w = dma("sy", sp_sg[chb, :, :], sg_chb[:, :])
            used_at("sg_chb", "sS", w)
        used_at(f"wbuf{si % 2}", "sT", cT.v)
        load_gate_strip(si + 2)

    for nm, (dstw, srcw, rows) in {"wout": (wout_bf, Woutd, INTER),
                                   "wg0": (wg0_bf, Wg0, HID),
                                   "wg1": (wg1_bf, Wg1, HID),
                                   "w2": (w2_bf, W2d, FFH)}.items():
        for r0 in range(0, rows, 1024):
            dma("gp", dstw[r0:r0 + 1024, :], srcw[r0:r0 + 1024, :], semname="sW")
        wcast[nm] = cW.v

    sbclose("sg_chb", "xstage", "diag3", "diag2", "diag1", "diag0", "xchb",
            "qbuf", "tnbuf", "vbuf", "xbc_slab", "wbuf1", "wbuf0")
    sbclose("xlnT")
    phase_gate()

    if KSTOP == "2c":
        return finish(nc, Q, sem, cS, ctr, cc_in, cc_out, ctx)
    tpeer = [sb(f"tpeer{i}", [128, INTER]) for i in range(2)]
    gtmp = sb("gtmp", [128, INTER])

    op("ve", "memset", globalC[:, :], 0.0)
    v_g = bump("ve")
    wait("sy", "sC", v_cc)
    wpc = 0
    for k in (1, 2):
        wpc = dma("sy", pcol[:, k:k + 1], cc_out[129 * k + 128:129 * k + 129, 0:64].transpose([1, 0]))
    tld = []
    for jj in range(2):
        tld.append(dma("sy", tpeer[jj % 2][:, :], cc_out[129 * jj: 129 * jj + 128, :]))
    wait("ve", "sS", wpc)
    for k in (1, 2):
        op("ve", "tensor_scalar_add", pcol[:, k:k + 1], pcol[:, k:k + 1], -1.0)
        op("ve", "tensor_scalar", pcol[:, k:k + 1], pcol[:, k:k + 1],
           cmask_s[:, 3 + (k - 1):3 + k], 1.0, ALU.mult, ALU.add)
    op("ve", "tensor_scalar", coefc[:, 2:3], ones64f[:, 0:1], cmask_s[:, 2:3], None, ALU.mult)
    op("ve", "tensor_scalar", coefc[:, 1:2], pcol[:, 2:3], cmask_s[:, 1:2], None, ALU.mult)
    op("ve", "tensor_mul", coefc[:, 0:1], pcol[:, 1:2], pcol[:, 2:3])
    op("ve", "tensor_scalar", coefc[:, 0:1], coefc[:, 0:1], cmask_s[:, 0:1], None, ALU.mult)
    v_coef = bump("ve")
    for jj in range(3):
        wait("ve", "sV", v_coef)
        guard("ve", "diagtmp")
        op("ve", "tensor_tensor", diagtmp[:, :], identf[0:64, 0:64],
           coefc[:, jj:jj + 1].broadcast_to([64, 64]), ALU.mult)
        v_dg = bump("ve")
        wait("te", "sV", v_dg)
        guard("te", "p_ms")
        op("te", "matmul", p_ms[:, 0:64], ones64f[:, :], diagtmp[:, :], start=True, stop=True)
        vt = bump("te")
        used_at("diagtmp", "sT", vt)
        wait("ac", "sT", vt)
        guard("ac", "cbc")
        op("ac", "activation", cbc[:, :], p_ms[:, 0:64], AF.Copy)
        va = bump("ac")
        used_at("p_ms", "sA", va)
        wait("gp", "sA", va)
        wait("gp", "sS", tld[jj])
        guard("gp", "gtmp")
        op("gp", "tensor_tensor", gtmp[:, :], tpeer[jj % 2][:, :],
           cbc[:, 0:NH].unsqueeze(2).broadcast_to([128, NH, HD]), ALU.mult)
        vg = bump("gp")
        used_at("cbc", "sP", vg)
        used_at(f"tpeer{jj % 2}", "sP", vg)
        wait("ve", "sP", vg)
        op("ve", "tensor_add", globalC[:, :], globalC[:, :], gtmp[:, :])
        v_g = bump("ve")
        used_at("gtmp", "sV", v_g)
        if jj + 2 < 3:
            guard("sy", f"tpeer{(jj + 2) % 2}")
            tld.append(dma("sy", tpeer[(jj + 2) % 2][:, :], cc_out[129 * (jj + 2): 129 * (jj + 2) + 128, :]))
    v_glob = v_g
    if TAPS:
        wait("sy", "sV", v_glob)
        dma("sy", taps["t_glob"], globalC[:, :])
    sbclose("gtmp", "tpeer1", "tpeer0")
    phase_gate()

    if KSTOP == "3":
        return finish(nc, Q, sem, cS, ctr, cc_in, cc_out, ctx)
    psclose("p_gate1", "p_gate0", "p_st1", "p_st0", "p_trf")
    phase_gate()
    p_df = psopen("p_df", [128, 512])
    p_ia = [psopen(f"p_ia{i}", [128, 512]) for i in range(2)]
    p_ie = [psopen(f"p_ie{i}", [128, 512]) for i in range(2)]

    diffR = sb("diffR", [97, NH * L])
    w_diffR = dma("sy", diffR[:, :], diffR_d)
    Ebuf = sb("Ebuf", [128, NH * L], BF16)
    z2buf = Ebuf[:, 0:NCHB_X * 128]
    Mt = sb("Mt", [128, NH * L], BF16)
    Gt_m = sb("Gt_m", [128, 128], BF16)
    Dpbc = sb("Dpbc", [128, NH], BF16)
    Gsc = sb("Gsc", [128, INTER], BF16)
    xdt = sb("xdt", [128, INTER], BF16)
    xDb = sb("xDb", [128, INTER], BF16)
    cbb = [sb(f"cbb{i}", [128, INTER], BF16) for i in range(2)]
    sgb = [sb(f"sgb{i}", [128, NCHB_X * 128], BF16) for i in range(2)]
    ybuf = sb("ybuf", [128, INTER], BF16)
    zbuf = Mt[:, 0:NCHB_X * 128]
    ev_t1 = sb("ev_t1", [128, 512])

    xT2, cb2, sg2 = [], [], []

    def load_p4(c):
        if c < NCH:
            guard("sy", f"xTb{c % 2}@p4")
            xT2.append(dma("sy", xTb[c % 2][:, :], sp_xT[c, :, :]))
            guard("gp", f"cbb{c % 2}")
            cb2.append(dma("gp", cbb[c % 2][:, :], sp_carry[c, :, :]))
            guard("sy", f"sgb{c % 2}")
            sg2.append(dma("sy", sgb[c % 2][:, :], sp_sg[:, :, c * L:(c + 1) * L].transpose([1, 0, 2])))
        else:
            xT2.append(0); cb2.append(0); sg2.append(0)

    for c in range(2):
        load_p4(c)
    w_p4pre = cS.v

    for c in range(NCH):
        wait("gp", "sS", w_p4pre)
        wait("gp", "sV", v_cA)
        guard("gp", "diffR64")
        w_fl = dma("gp", diffR[64:65, :], diffL[0:64, c * L:(c + 1) * L])
        guard("te", "p_ms")
        op("te", "matmul", p_ms[:, :], B_FM[:, c * L:(c + 1) * L], C_FM[:, c * L:(c + 1) * L],
           start=True, stop=True)
        vt = bump("te")
        wait("ve", "sT", vt)
        guard("ve", "Gt_m")
        op("ve", "tensor_tensor", Gt_m[:, :], p_ms[:, :], triu_f[:, :], ALU.mult)
        v_gt = bump("ve")
        used_at("p_ms", "sV", v_gt)
        wait("ve", "sV", v_dp)
        guard("ve", "diagtmp")
        op("ve", "tensor_tensor", diagtmp[:, :], identf[0:64, 0:64],
           Dprod[:, c:c + 1].broadcast_to([64, 64]), ALU.mult)
        v_dg = bump("ve")
        wait("te", "sV", v_dg)
        guard("te", "p_ms")
        op("te", "matmul", p_ms[:, 0:64], ones64f[:, :], diagtmp[:, :], start=True, stop=True)
        vt = bump("te")
        used_at("diagtmp", "sT", vt)
        wait("ve", "sT", vt)
        guard("ve", "Dpbc")
        op("ve", "tensor_copy", Dpbc[:, :], p_ms[:, 0:64])
        v_dpb = bump("ve")
        used_at("p_ms", "sV", v_dpb)
        wait("gp", "sV", v_dpb)
        wait("gp", "sV", v_glob)
        guard("gp", "Gsc")
        op("gp", "tensor_tensor", Gsc[:, :], globalC[:, :],
           Dpbc[:, 0:NH].unsqueeze(2).broadcast_to([128, NH, HD]), ALU.mult)
        v_gsc = bump("gp")
        used_at("Dpbc", "sP", v_gsc)

        wait("gp", "sS", xT2[c])
        guard("gp", "xdt")
        op("gp", "tensor_tensor", xdt[:, :], xTb[c % 2][:, :],
           dtTM[:, c * NH:(c + 1) * NH].unsqueeze(2).broadcast_to([128, NH, HD]), ALU.mult)
        guard("gp", "xDb")
        op("gp", "tensor_tensor", xDb[:, :], xTb[c % 2][:, :],
           Dbc_s[:, 0:NH].unsqueeze(2).broadcast_to([128, NH, HD]), ALU.mult)
        v_xops = bump("gp")
        used_at(f"xTb{c % 2}@p4", "sP", v_xops)
        if TAPS and c == 5:
            wait("sy", "sP", v_xops)
            dma("sy", taps["t_xbc"], xTb[c % 2][:, :])

        guard("ac", "Ebuf")
        for dsl in range(NFB):
            wait("te", "sG", w_fl)
            wait("te", "sS", w_p4pre)
            wait("te", "sV", v_cA)
            guard("te", "p_df")
            for half in range(2):
                op("te", "matmul", p_df[:, half * 256:(half + 1) * 256],
                   diffL[:, c * L:(c + 1) * L],
                   diffR[:, dsl * 512 + half * 256: dsl * 512 + (half + 1) * 256],
                   start=True, stop=True)
            vt = bump("te")
            if dsl == NFB - 1:
                used_at("diffR64", "sT", vt)
            wait("ac", "sT", vt)
            op("ac", "activation", Ebuf[:, dsl * 512:(dsl + 1) * 512], p_df[:, :], AF.Exp)
            va = bump("ac")
            used_at("p_df", "sA", va)
        v_E = cA.v
        wait("ve", "sA", v_E)
        if TAPS and c == 0:
            wait("sy", "sG", w_fl)
            wait("sy", "sS", w_diffR)
            wait("sy", "sA", v_E)
            wsn = dma("sy", taps["t_misc"][64:65, 0:16], diffR[64:65, 0:16])
            wsn = dma("sy", taps["t_misc"][0:1, 16:32], diffL[0:1, 0:16])
            used_also("diffR64", "sS", wsn)
        wait("ve", "sV", v_gt)
        guard("ve", "MtL")
        op("ve", "tensor_tensor", Mt[:, 0:NH * L // 2], Ebuf[:, 0:NH * L // 2],
           Gt_m[:, :].unsqueeze(1).broadcast_to([128, NH // 2, 128]), ALU.mult)
        v_mt1 = bump("ve")
        wait("gp", "sA", v_E)
        wait("gp", "sV", v_gt)
        guard("gp", "MtH")
        op("gp", "tensor_tensor", Mt[:, NH * L // 2:], Ebuf[:, NH * L // 2:],
           Gt_m[:, :].unsqueeze(1).broadcast_to([128, NH // 2, 128]), ALU.mult)
        v_mt2 = bump("gp")
        used_at("Gt_m", "sP", v_mt2)

        wait("te", "sP", v_gsc)
        wait("te", "sP", v_xops)
        wait("te", "sP", v_mt2)
        wait("te", "sV", v_mt1)
        wait("te", "sG", cb2[c])
        for q8 in range(8):
            par = q8 % 2
            pa, pe_ = f"p_ia{par}", f"p_ie{par}"
            guard("te", pa)
            guard("te", pe_)
            op("te", "matmul", p_ia[par][:, :], identb[:, :], xDb[:, q8 * 512:(q8 + 1) * 512],
               start=True, stop=False)
            for hh in range(8):
                h = q8 * 8 + hh
                op("te", "matmul", p_ia[par][:, hh * 64:(hh + 1) * 64], Mt[:, h * 128:(h + 1) * 128],
                   xdt[:, h * 64:(h + 1) * 64], start=False, stop=(hh == 7))
            op("te", "matmul", p_ie[par][:, :], C_FM[:, c * L:(c + 1) * L],
               cbb[c % 2][:, q8 * 512:(q8 + 1) * 512], start=True, stop=False)
            op("te", "matmul", p_ie[par][:, :], C_FM[:, c * L:(c + 1) * L],
               Gsc[:, q8 * 512:(q8 + 1) * 512], start=False, stop=True)
            vt = bump("te")
            wait("ve", "sT", vt)
            guard("ve", "ev_t1")
            op("ve", "tensor_tensor", ev_t1[:, :], p_ie[par][:, :],
               uall[:, c * NH + q8 * 8: c * NH + (q8 + 1) * 8].unsqueeze(2).broadcast_to([128, 8, 64]),
               ALU.mult)
            if q8 == 0:
                guard("ve", "ybuf")
            op("ve", "tensor_add", ybuf[:, q8 * 512:(q8 + 1) * 512], ev_t1[:, :], p_ia[par][:, :])
            vv = bump("ve")
            used_at(pa, "sV", vv)
            used_at(pe_, "sV", vv)
        v_y = cV.v
        v_te_y = cT.v
        used_at("MtL", "sT", v_te_y)
        used_at("MtH", "sT", v_te_y)
        used_at("xdt", "sT", v_te_y)
        used_at("xDb", "sT", v_te_y)
        used_at("Gsc", "sT", v_te_y)
        used_at(f"cbb{c % 2}", "sT", v_te_y)
        if TAPS and c == 0:
            wait("ve", "sG", cb2[0])
            wait("ve", "sP", v_mt2)
            op("ve", "tensor_copy", dbg[:, 32:48], Mt[:, 0:16])
            op("ve", "tensor_copy", dbg[:, 48:64], Ebuf[:, 0:16])
            op("ve", "tensor_copy", dbg[:, 64:72], uall[:, 0:8])
            op("ve", "tensor_copy", dbg[:, 72:80], xDb[:, 0:8])
            vv = bump("ve")
            for _nm in ("xDb", "MtL", "Ebuf"):
                used_also(_nm, "sV", vv)
        if TAPS and c == 5:
            wait("sy", "sV", v_y)
            dma("sy", taps["t_y"], ybuf[:, :])

        wait("te", "sV", v_y)
        for chb in range(NCHB_X):
            pt, pname = (p_trb, "p_trb") if chb % 2 == 0 else (p_trb2, "p_trb2")
            guard("te", pname)
            op("te", "transpose", pt[:, :], ybuf[:, chb * 128:(chb + 1) * 128], identb[:, :])
            vt = bump("te")
            wait("ve", "sT", vt)
            wait("ve", "sS", sg2[c])
            if chb == 0:
                guard("ve", "MtL")
            op("ve", "tensor_tensor", zbuf[:, chb * 128:(chb + 1) * 128], pt[:, :],
               sgb[c % 2][:, chb * 128:(chb + 1) * 128], ALU.mult)
            vv = bump("ve")
            used_at(pname, "sV", vv)
        used_at("ybuf", "sT", cT.v)
        used_at(f"sgb{c % 2}", "sV", cV.v)
        guard("ve", "Ebuf")
        op("ve", "tensor_mul", z2buf, zbuf[:, :], zbuf[:, :])
        v_z2 = bump("ve")
        wait("te", "sV", v_z2)
        guard("te", "p_ms")
        for chb in range(NCHB_X):
            op("te", "matmul", p_ms[0:1, :], ones128b[:, 0:1], z2buf[:, chb * 128:(chb + 1) * 128],
               start=(chb == 0), stop=(chb == NCHB_X - 1))
        vt = bump("te")
        used_at("Ebuf", "sT", vt)
        wait("ve", "sT", vt)
        guard("ve", "vrow")
        op("ve", "tensor_copy", vrow[0:1, :], p_ms[0:1, :])
        v_vr = bump("ve")
        used_at("p_ms", "sV", v_vr)
        wait("te", "sV", v_vr)
        guard("te", "p_df")
        op("te", "transpose", p_df[:, 0:1], vrow[0:1, :], identf[0:1, 0:1])
        vt = bump("te")
        used_at("vrow", "sT", vt)
        wait("ve", "sT", vt)
        op("ve", "tensor_copy", varz[:, c:c + 1], p_df[:, 0:1])
        vv = bump("ve")
        used_at("p_df", "sV", vv)
        wait("sy", "sV", v_z2)
        w_zs = dma("sy", sp_z[:, :, c * L:(c + 1) * L].transpose([1, 0, 2]), zbuf[:, :])
        used_also("MtL", "sS", w_zs)
        if TAPS and c == 5:
            dma("sy", taps["t_z"], zbuf[:, :])
        load_p4(c + 2)

    hz1 = hscr1; hz2 = hscr2
    op("ve", "tensor_scalar_mul", varz[:, :], varz[:, :], 1.0 / INTER)
    op("ve", "tensor_scalar_add", varz[:, :], varz[:, :], EPS)
    emit_heron("ve", rstdz[:, :], varz[:, :], hz1[:, 0:NCH], hz2[:, 0:NCH])
    v_rz = bump("ve")

    sbclose("ev_t1", "ybuf", "sgb1", "sgb0", "cbb1", "cbb0",
            "xDb", "xdt", "Gsc", "Dpbc", "Gt_m", "Mt", "Ebuf", "diffR")
    sbclose("globalC", "xTb1", "xTb0", "uall", "dtTM", "diffL", "dtv", "C_FM", "B_FM")
    phase_gate()

    if KSTOP == "4":
        return finish(nc, Q, sem, cS, ctr, cc_in, cc_out, ctx)
    psclose("p_ie1", "p_ie0", "p_ia1", "p_ia0", "p_df")
    phase_gate()
    p_o = [psopen(f"p_o{i}", [128, 512]) for i in range(2)]

    wout_sb = sb("wout_sb", [128, NCHB_X * HID], BF16)
    zc = [sb(f"zc{i}", [128, NCHB_X * 128], BF16) for i in range(2)]
    xrow = [sb(f"xrow{i}", [128, HID]) for i in range(2)]
    x2buf = sb("x2buf", [128, HID])
    o1buf = sb("o1buf", [128, 512])
    x2ln = sb("x2ln", [128, HID], BF16)
    lnstage = sb("lnstage", [128, NFB * 128], FP8)

    wait("sy", "sW", wcast["wout"])
    wwo = 0
    for kb in range(NCHB_X):
        wwo = dma("sy", wout_sb[:, kb * HID:(kb + 1) * HID], wout_bf[kb * 128:(kb + 1) * 128, :])
    wait("ve", "sS", wwo)
    for kb in range(NCHB_X):
        op("ve", "tensor_scalar", wout_sb[:, kb * HID:(kb + 1) * HID],
           wout_sb[:, kb * HID:(kb + 1) * HID], normw_s[:, kb:kb + 1], None, ALU.mult)
    v_wo = bump("ve")

    zld, xld = [], []

    def load_p5(c):
        if c < NCH:
            guard("sy", f"zc{c % 2}")
            zld.append(dma("sy", zc[c % 2][:, :], sp_z[:, :, c * L:(c + 1) * L].transpose([1, 0, 2])))
            guard("sy", f"xrow{c % 2}")
            xld.append(dma("sy", xrow[c % 2][:, :], xh[3 + c * L: 3 + (c + 1) * L, :]))
        else:
            zld.append(0); xld.append(0)

    for c in range(2):
        load_p5(c)

    for c in range(NCH):
        wait("te", "sV", v_wo)
        wait("te", "sS", zld[c])
        for jsl in range(4):
            par = jsl % 2
            pname = f"p_o{par}"
            guard("te", pname)
            for kb in range(NCHB_X):
                op("te", "matmul", p_o[par][:, :], zc[c % 2][:, kb * 128:(kb + 1) * 128],
                   wout_sb[:, kb * HID + jsl * 512: kb * HID + (jsl + 1) * 512],
                   start=(kb == 0), stop=(kb == NCHB_X - 1))
            vt = bump("te")
            wait("ac", "sT", vt)
            wait("ac", "sV", v_rz)
            guard("ac", "o1buf")
            op("ac", "activation", o1buf[:, :], p_o[par][:, :], AF.Copy, scale=rstdz[:, c:c + 1])
            va = bump("ac")
            used_at(pname, "sA", va)
            wait("ve", "sA", va)
            wait("ve", "sS", xld[c])
            if jsl == 0:
                guard("ve", "x2buf")
            op("ve", "tensor_add", x2buf[:, jsl * 512:(jsl + 1) * 512], o1buf[:, :],
               xrow[c % 2][:, jsl * 512:(jsl + 1) * 512])
            vv = bump("ve")
            used_at("o1buf", "sV", vv)
        v_x2 = cV.v
        used_at(f"zc{c % 2}", "sT", cT.v)
        used_at(f"xrow{c % 2}", "sV", v_x2)
        if TAPS and c == 5:
            wait("sy", "sV", v_x2)
            dma("sy", taps["t_x2"], x2buf[:, :])
        wait("sy", "sV", v_x2)
        w_x2 = dma("sy", sp_x2[c, :, :], x2buf[:, :])
        op("ve", "reduce_sum", stat2_m[:, c:c + 1], x2buf[:, :], mybir.AxisListType.X)
        vv_r2 = bump("ve")
        wait("ac", "sV", v_x2)
        guard("ac", "sq_scr")
        op("ac", "activation", sq_scr[:, :], x2buf[:, :], AF.Square, accum_out=stat2_q[:, c:c + 1])
        va = bump("ac")
        used_at("x2buf", "sA", va)
        used_also("x2buf", "sV", vv_r2)
        used_also("x2buf", "sS", w_x2)
        load_p5(c + 2)
    v_stats2 = cA.v
    if KSTOP == "5a":
        return finish(nc, Q, sem, cS, ctr, cc_in, cc_out, ctx)

    wait("ve", "sA", v_stats2)
    wait("ve", "sV", cV.v)
    op("ve", "tensor_scalar_mul", stat2_m[:, :], stat2_m[:, :], 1.0 / HID)
    op("ve", "tensor_scalar_mul", stat2_q[:, :], stat2_q[:, :], 1.0 / HID)
    op("ve", "tensor_mul", hz1[:, 0:NCH], stat2_m[:, :], stat2_m[:, :])
    op("ve", "tensor_sub", stat2_q[:, :], stat2_q[:, :], hz1[:, 0:NCH])
    op("ve", "tensor_scalar_add", stat2_q[:, :], stat2_q[:, :], EPS)
    emit_heron("ve", rstd2[:, :], stat2_q[:, :], hz1[:, 0:NCH], hz2[:, 0:NCH])
    op("ve", "tensor_mul", bias2[:, :], stat2_m[:, :], rstd2[:, :])
    op("ve", "tensor_scalar_mul", bias2[:, :], bias2[:, :], -1.0)
    v_r2 = bump("ve")

    if TAPS:
        wait("sy", "sV", cV.v)
        dma("sy", taps["t_misc"][:, 32:80], dbg[:, 32:80])
    if KSTOP == "5s":
        return finish(nc, Q, sem, cS, ctr, cc_in, cc_out, ctx)
    x2ld = []

    def load_p5b(c):
        if c < NCH:
            guard("sy", f"xrow{c % 2}")
            x2ld.append(dma("sy", xrow[c % 2][:, :], sp_x2[c, :, :]))
        else:
            x2ld.append(0)

    for c in range(2):
        load_p5b(c)
    for c in range(NCH):
        wait("ac", "sV", v_r2)
        wait("ac", "sS", x2ld[c])
        guard("ac", "x2ln")
        op("ac", "activation", x2ln[:, :], xrow[c % 2][:, :], AF.Identity,
           bias=bias2[:, c:c + 1], scale=rstd2[:, c:c + 1])
        va = bump("ac")
        used_at(f"xrow{c % 2}", "sA", va)
        wait("te", "sA", va)
        for fb in range(NFB):
            pt, pname = (p_trb, "p_trb") if fb % 2 == 0 else (p_trb2, "p_trb2")
            guard("te", pname)
            op("te", "transpose", pt[:, :], x2ln[:, fb * 128:(fb + 1) * 128], identb[:, :])
            vt = bump("te")
            wait("ve", "sT", vt)
            if fb == 0:
                guard("ve", "lnstage")
            op("ve", "tensor_copy", lnstage[:, fb * 128:(fb + 1) * 128], pt[:, :])
            vv = bump("ve")
            used_at(pname, "sV", vv)
        used_at("x2ln", "sT", cT.v)
        wait("sy", "sV", cV.v)
        w = dma("sy", sp_x2lnT[:, :, c * L:(c + 1) * L].transpose([1, 0, 2]), lnstage[:, :])
        used_at("lnstage", "sS", w)
        load_p5b(c + 2)

    if KSTOP == "5b":
        return finish(nc, Q, sem, cS, ctr, cc_in, cc_out, ctx)
    sbclose("lnstage", "x2ln", "o1buf", "x2buf", "xrow1", "xrow0", "zc1", "zc0", "wout_sb")
    phase_gate()

    if KSTOP == "5":
        return finish(nc, Q, sem, cS, ctr, cc_in, cc_out, ctx)
    psclose("p_o1", "p_o0", "p_ms", "p_trb2", "p_trb")
    phase_gate()
    p_fg = psopen("p_fg", [128, TT1])
    p_ff = psopen("p_ff", [128, TT1])

    x2lt = sb("x2lt", [128, NFB * TT1], FP8)
    act2 = sb("act2", [128, NJB * TT1], FP8)
    JW = 8
    wgb = [sb(f"wgb{i}", [128, 2 * NFB * 128 * JW], FP8) for i in range(2)]
    w2b = [sb(f"w2b{i}", [128, 4 * 512], FP8) for i in range(2)]
    xgb = sb("xgb", [128, TT1], BF16)
    f1b = sb("f1b", [128, TT1], BF16)
    gsc1 = sb("gsc1", [128, TT1], BF16)
    gsc2 = gsc1
    obuf = sb("obuf", [128, 512])
    fob = sb("fob", [128, 512], BF16)
    x2sl = [sb(f"x2sl{i}", [128, 512]) for i in range(2)]

    for u in range(2):
        guard("sy", "x2lt")
        w_xt = dma("sy", x2lt[:, :], sp_x2lnT[:, :, u * TT1:(u + 1) * TT1].transpose([1, 0, 2]))
        wait("sy", "sW", max(wcast["wg0"], wcast["wg1"]))
        wait("gp", "sW", max(wcast["wg0"], wcast["wg1"]))
        wait("te", "sV", cV.v)
        wait("te", "sA", cA.v)
        wg_ld = []
        NJG1 = NJB // JW
        CW = JW * 128

        def load_wg(J):
            if J < NJG1:
                guard("sy", f"wgb{J % 2}")
                guard("gp", f"wgb{J % 2}")
                w1 = dma("sy", wgb[J % 2][:, 0:NFB * CW],
                         wg0_bf[:, J * CW:(J + 1) * CW].rearrange("(fb p) c -> p fb c", p=128))
                w2_ = dma("gp", wgb[J % 2][:, NFB * CW:],
                          wg1_bf[:, J * CW:(J + 1) * CW].rearrange("(fb p) c -> p fb c", p=128))
                wg_ld.append((w1, w2_))
            else:
                wg_ld.append((0, 0))

        if TAPS and u == 0:
            dma("sy", taps["t_p6a"], sp_x2lnT[:, :, 0:64].transpose([1, 0, 2]))
        for J in range(2):
            load_wg(J)
        for jb in range(NJB):
            J, jj = jb // JW, jb % JW
            Jw = 1 if J == 0 and NJG1 > 1 else J
            wait("te", "sS", max(wg_ld[Jw][0], w_xt))
            wait("te", "sG", wg_ld[Jw][1])
            guard("te", "p_fg")
            guard("te", "p_ff")
            for half in range(2):
                for fb in range(NFB):
                    op("te", "matmul", p_fg[:, half * 512:(half + 1) * 512],
                       wgb[J % 2][:, fb * CW + jj * 128:fb * CW + (jj + 1) * 128],
                       x2lt[:, fb * TT1 + half * 512: fb * TT1 + (half + 1) * 512],
                       start=(fb == 0), stop=(fb == NFB - 1))
            for half in range(2):
                for fb in range(NFB):
                    op("te", "matmul", p_ff[:, half * 512:(half + 1) * 512],
                       wgb[J % 2][:, (NFB + fb) * CW + jj * 128:(NFB + fb) * CW + (jj + 1) * 128],
                       x2lt[:, fb * TT1 + half * 512: fb * TT1 + (half + 1) * 512],
                       start=(fb == 0), stop=(fb == NFB - 1))
            vt = bump("te")
            if jj == JW - 1:
                used_at(f"wgb{J % 2}", "sT", vt)
            wait("ac", "sT", vt)
            guard("ac", "xgb")
            op("ac", "activation", xgb[:, :], p_fg[:, :], AF.Copy)
            va = bump("ac")
            used_at("p_fg", "sA", va)
            wait("ve", "sT", vt)
            guard("ve", "f1b")
            op("ve", "tensor_copy", f1b[:, :], p_ff[:, :])
            v_f1 = bump("ve")
            used_at("p_ff", "sV", v_f1)
            wait("gp", "sA", va)
            guard("gp", "gsc1")
            op("gp", "tensor_mul", gsc1[:, :], xgb[:, :], xgb[:, :])
            op("gp", "tensor_scalar", gsc1[:, :], gsc1[:, :], GELU_C2, 1.0, ALU.mult, ALU.add)
            op("gp", "tensor_mul", gsc1[:, :], gsc1[:, :], xgb[:, :])
            vg = bump("gp")
            wait("ac", "sP", vg)
            op("ac", "activation", gsc2[:, :], gsc1[:, :], AF.Tanh, scale=GELU_C1)
            va2 = bump("ac")
            wait("ve", "sA", va2)
            wait("ve", "sV", v_f1)
            op("ve", "tensor_scalar", gsc2[:, :], gsc2[:, :], 0.5, 0.5, ALU.mult, ALU.add)
            op("ve", "tensor_mul", gsc2[:, :], gsc2[:, :], xgb[:, :])
            op("ve", "tensor_tensor", act2[:, jb * TT1:(jb + 1) * TT1], gsc2[:, :], f1b[:, :], ALU.mult)
            vv = bump("ve")
            used_at("gsc1", "sV", vv)
            used_at("xgb", "sV", vv)
            used_at("f1b", "sV", vv)
            if jj == JW - 1:
                load_wg(J + 2)
        v_act2 = cV.v
        if TAPS and u == 0:
            wait("sy", "sV", v_act2)
            dma("sy", taps["t_p6b"], act2[:, 0:1024])

        psclose(f"p_ff_u{u-1}" if u > 0 else "p_ff", f"p_fg_u{u-1}" if u > 0 else "p_fg")
        p_f2 = [psopen(f"p_f2_{u}_{i}", [128, 512]) for i in range(8)]
        wait("sy", "sW", wcast["w2"])
        wait("gp", "sW", wcast["w2"])
        wait("te", "sV", v_act2)
        NJG = 16
        for f2sl in range(4):
            w2_ld = []

            def load_w2(jg):
                if jg < NJG:
                    qn = "sy" if jg % 2 == 0 else "gp"
                    guard(qn, f"w2b{jg % 2}")
                    w2_ld.append(dma(qn, w2b[jg % 2][:, :],
                                     w2_bf[jg * 512:(jg + 1) * 512,
                                           f2sl * 512:(f2sl + 1) * 512].rearrange("(j p) c -> p j c", p=128)))
                else:
                    w2_ld.append(0)

            x2sll = []

            def load_x2sl(tb):
                if tb < 8:
                    guard("sy", f"x2sl{tb % 2}")
                    x2sll.append(dma("sy", x2sl[tb % 2][:, :],
                                     sp_x2[u * 8 + tb, :, f2sl * 512:(f2sl + 1) * 512]))
                else:
                    x2sll.append(0)

            for jg in range(2):
                load_w2(jg)
            for tb in range(2):
                load_x2sl(tb)
            for tb in range(8):
                guard("te", f"p_f2_{tb}")
            for jg in range(NJG):
                wait("te", "sS" if jg % 2 == 0 else "sG", w2_ld[jg])
                for ji in range(4):
                    jb = jg * 4 + ji
                    for tb in range(8):
                        op("te", "matmul", p_f2[tb][:, :],
                           act2[:, jb * TT1 + tb * 128: jb * TT1 + (tb + 1) * 128],
                           w2b[jg % 2][:, ji * 512:(ji + 1) * 512],
                           start=(jb == 0), stop=(jb == NJB - 1))
                vt_g = bump("te")
                used_at(f"w2b{jg % 2}", "sT", vt_g)
                load_w2(jg + 2)
            vt = cT.v
            for tb in range(8):
                wait("ve", "sT", vt)
                wait("ve", "sS", x2sll[tb])
                guard("ve", "obuf")
                op("ve", "tensor_copy", fob[:, :], p_f2[tb][:, :])
                op("ve", "tensor_add", obuf[:, :], fob[:, :], x2sl[tb % 2][:, :])
                vv = bump("ve")
                used_at(f"p_f2_{tb}", "sV", vv)
                used_at(f"x2sl{tb % 2}", "sV", vv)
                wait("sy", "sV", vv)
                w_o = dma("sy", out[u * TT1 + tb * 128: u * TT1 + (tb + 1) * 128,
                                    f2sl * 512:(f2sl + 1) * 512], obuf[:, :])
                used_at("obuf", "sS", w_o)
                load_x2sl(tb + 2)
        psclose(*[f"p_f2_{u}_{i}" for i in range(7, -1, -1)])
        phase_gate()
        p_fg = psopen(f"p_fg_u{u}", [128, TT1])
        p_ff = psopen(f"p_ff_u{u}", [128, TT1])
        used_at("p_fg", "sA", 0)
        used_at("p_ff", "sV", 0)
        used_at("x2lt", "sT", cT.v)
        used_at("act2", "sT", cT.v)
    psclose("p_ff_u1", "p_fg_u1")

    return finish(nc, Q, sem, cS, ctr, cc_in, cc_out, ctx)


def finish(nc, Q, sem, cS, ctr, cc_in, cc_out, ctx):
    Q["sy"].append(("wait", "sS", cS.v))

    def run_queue(e, q):
        for item in Q[q]:
            kind = item[0]
            if kind == "op":
                inst = getattr(e, item[1])(*item[2], **item[3])
                for sname in item[4]:
                    inst = inst.then_inc(sem[sname], 1)
            elif kind == "wait":
                e.wait_ge(sem[item[1]], item[2])
            elif kind == "dma":
                inst = e.dma_start(out=item[1], in_=item[2]).then_inc(sem[item[3]], 16)
                for sname in item[4]:
                    inst = inst.then_inc(sem[sname], 1)
            elif kind == "cc":
                inst = e.collective_compute(
                    "AllGather", mybir.AluOpType.bypass, replica_groups=[[0, 1, 2, 3], [4, 5, 6, 7]],
                    ins=[cc_in.opt()], outs=[cc_out.opt()]).then_inc(sem["sC"], 1)
                for sname in item[-1]:
                    inst = inst.then_inc(sem[sname], 1)

    ALU_ = mybir.AluOpType
    with nc.Block() as block:
        @block.sync
        def _(e):
            run_queue(e, "sy")

        @block.gpsimd
        def _(e):
            run_queue(e, "gp")

        @block.vector
        def _(e):
            run_queue(e, "ve")

        @block.tensor
        def _(e):
            run_queue(e, "te")

        @block.scalar
        def _(e):
            run_queue(e, "ac")

    ctx.close()
    return nc


def _host_consts():
    bf = ml_dtypes.bfloat16
    identf = np.eye(128, dtype=np.float32)
    identb = np.eye(128, dtype=bf)
    triu = np.triu(np.ones((128, 128), np.float32))
    ones128b = np.ones((128, 1), dtype=bf)
    diffL = np.zeros((97, 2048), np.float32)
    diffL[64, :] = 1.0
    s_idx = np.arange(128)
    masklhs = np.zeros((32, 128), np.float32)
    for b in range(32):
        masklhs[b, s_idx // 4 == b] = 30000.0
    diffL[65:97, :] = np.tile(masklhs, (1, 16))
    diffR = np.zeros((97, NH * L), np.float32)
    for hp in range(64):
        diffR[hp, hp * L:(hp + 1) * L] = -1.0
    l_idx = np.arange(L)
    for b in range(32):
        row = np.zeros((L,), np.float32)
        row[l_idx // 4 < b] = -1.0
        diffR[65 + b, :] = np.tile(row, NH)
    return identf, identb, triu, ones128b, diffL, diffR


def kernel(L0_attn_hidden_states, L0_attn_W_in, L0_attn_conv_w, L0_attn_conv_b,
           L0_attn_dt_bias, L0_attn_A, L0_attn_D, L0_attn_norm_weight,
           L0_attn_W_out, L0_ffn_w_gating, L0_ffn_w_linear):
    bf = ml_dtypes.bfloat16
    x = np.ascontiguousarray(np.asarray(L0_attn_hidden_states, dtype=np.float32))
    W_in = np.ascontiguousarray(np.asarray(L0_attn_W_in, dtype=np.float32))
    conv_w = np.asarray(L0_attn_conv_w, dtype=np.float32)
    conv_b = np.asarray(L0_attn_conv_b, dtype=np.float32)
    dt_bias = np.asarray(L0_attn_dt_bias, dtype=np.float32)
    A = np.asarray(L0_attn_A, dtype=np.float32)
    D = np.asarray(L0_attn_D, dtype=np.float32)
    norm_w = np.asarray(L0_attn_norm_weight, dtype=np.float32)
    W_out = np.ascontiguousarray(np.asarray(L0_attn_W_out, dtype=np.float32))
    w_g = np.asarray(L0_ffn_w_gating, dtype=np.float32)
    w_l = np.ascontiguousarray(np.asarray(L0_ffn_w_linear, dtype=np.float32))

    identf, identb, triu, ones128b, diffL, diffR = _host_consts()
    wconv = np.ascontiguousarray(conv_w.reshape(NCHB_BC, 128, KC).transpose(1, 0, 2).reshape(128, NCHB_BC * KC))
    convb = np.ascontiguousarray(conv_b.reshape(NCHB_BC, 128).T)
    normw = np.ascontiguousarray(norm_w.reshape(NCHB_X, 128).T)
    Dbc = np.ascontiguousarray(np.tile(D[None, :], (128, 1)).astype(bf))
    Wg0 = np.ascontiguousarray(w_g[0])
    Wg1 = np.ascontiguousarray(w_g[1])

    in_maps = []
    for k in range(8):
        b, q = k // 4, k % 4
        xhv = np.zeros((TH, HID), np.float32)
        t0 = q * T
        if q > 0:
            xhv[0:3] = x[b, t0 - 3:t0]
        xhv[3:] = x[b, t0:t0 + T]
        cm = np.zeros((8,), np.float32)
        for j in range(3):
            cm[j] = 1.0 if j < q else 0.0
        for kk in (1, 2):
            cm[3 + kk - 1] = 1.0 if kk < q else 0.0
        cmask = np.ascontiguousarray(np.tile(cm[None, :], (64, 1)))
        in_maps.append({
            "xh": np.ascontiguousarray(xhv), "W_in": W_in, "Wg0": Wg0, "Wg1": Wg1,
            "W2d": w_l, "Woutd": W_out, "wconv": wconv, "convb": convb,
            "dtbias": np.ascontiguousarray(dt_bias[:, None]),
            "Acol": np.ascontiguousarray(A[:, None]), "Dbc_d": Dbc,
            "normw": normw, "cmask": cmask,
            "identf_d": identf, "identb_d": identb, "triu_d": triu,
            "ones128b_d": ones128b, "diffL_d": diffL, "diffR_d": diffR,
        })

    nc = build_nc()
    res = run_bass_kernel_spmd(nc, in_maps, core_ids=list(range(8)))
    outp = np.zeros((2, 8192, HID), np.float32)
    for k in range(8):
        b, q = k // 4, k % 4
        outp[b, q * T:(q + 1) * T] = res.results[k]["out"]
    if TAPS:
        kernel.taps = [res.results[k] for k in range(8)]
    return outp
